# version 03
# baseline (speedup 1.0000x reference)
import numpy as np
from contextlib import ExitStack

import concourse.bass as bass
import concourse.mybir as mybir
from concourse.bass_utils import run_bass_kernel_spmd

F32 = mybir.dt.float32
BF16 = mybir.dt.bfloat16
I32 = mybir.dt.int32
AF = mybir.ActivationFunctionType
ALU = mybir.AluOpType

ENGINES = ("tensor", "vector", "scalar", "gpsimd", "sync")



class Op:
    __slots__ = ("engine", "emit", "reads", "writes", "inc", "deps", "needs_inc",
                 "count", "tag", "lane")

    def __init__(self, engine, emit, reads, writes, inc, tag="", lane=None):
        self.engine = engine
        self.emit = emit
        self.reads = reads
        self.writes = writes
        self.inc = inc
        self.deps = []
        self.needs_inc = False
        self.count = None
        self.tag = tag
        self.lane = lane


NLANES = 4


class Prog:
    def __init__(self, nc):
        self.nc = nc
        self.ops = {e: [] for e in ENGINES}
        self.last_writer = {}
        self.readers = {}
        self.stack = ExitStack()
        self.sems = {}
        self.final_ops = []
        self.dma_rr = {e: 0 for e in ENGINES}

    def sbuf(self, name, shape, dtype):
        return self.stack.enter_context(self.nc.sbuf_tensor(name, shape, dtype))

    def psum(self, name, shape, dtype=F32):
        return self.stack.enter_context(self.nc.psum_tensor(name, shape, dtype))

    def add(self, engine, emit, r=(), w=(), dma=False, tag="", final=False):
        lane = None
        if dma:
            lane = self.dma_rr[engine] % NLANES
            self.dma_rr[engine] += 1
        op = Op(engine, emit, list(r), list(w), 16 if dma else 1, tag, lane)
        deps = []
        for k in op.reads:
            lw = self.last_writer.get(k)
            if lw is not None:
                deps.append(lw)
        for k in op.writes:
            lw = self.last_writer.get(k)
            if lw is not None:
                deps.append(lw)
            deps.extend(self.readers.get(k, ()))
        seen = set()
        for d in deps:
            if d is not op and id(d) not in seen:
                seen.add(id(d))
                op.deps.append(d)
        for k in op.reads:
            self.readers.setdefault(k, []).append(op)
        for k in op.writes:
            self.last_writer[k] = op
            self.readers[k] = []
        self.ops[engine].append(op)
        if final:
            self.final_ops.append(op)
        return op

    def finalize(self):
        for e in ENGINES:
            for op in self.ops[e]:
                for d in op.deps:
                    d.needs_inc = True
                if op.lane is not None:
                    op.needs_inc = True
        for op in self.final_ops:
            op.needs_inc = True
        counters = {}
        used = set()
        for e in ENGINES:
            for op in self.ops[e]:
                if op.needs_inc:
                    key = (e, op.lane)
                    counters[key] = counters.get(key, 0) + op.inc
                    op.count = counters[key]
                    used.add(key)
        for key in sorted(used, key=str):
            nm = f"sem_{key[0]}" + ("" if key[1] is None else f"_l{key[1]}")
            self.sems[key] = self.stack.enter_context(self.nc.semaphore(nm))

    def emit(self, engine, eng):
        watermark = {}
        lane_prev = {}
        for op in self.ops[engine]:
            need = {}
            for d in op.deps:
                k = (d.engine, d.lane)
                need[k] = max(need.get(k, 0), d.count)
            if op.lane is not None:
                k = (engine, op.lane)
                prev = lane_prev.get(op.lane)
                if prev is not None:
                    need[k] = max(need.get(k, 0), prev)
                lane_prev[op.lane] = op.count
            for k2, v in sorted(need.items(), key=str):
                if watermark.get(k2, 0) < v:
                    eng.wait_ge(self.sems[k2], v)
                    watermark[k2] = v
            instr = op.emit(eng)
            if op.needs_inc:
                if instr is None:
                    raise RuntimeError(f"op {op.tag} needs inc but emitted no instr")
                instr.then_inc(self.sems[(engine, op.lane)], op.inc)

    def emit_all(self):
        self.finalize()
        with self.nc.Block() as blk:
            for e in ENGINES:
                if not self.ops[e]:
                    continue
                getattr(blk, e)(lambda eng, e=e: self.emit(e, eng))
        self.stack.close()




def bcast_const_col(p, name, value, parts=128):
    t = p.sbuf(name, [parts, 1], F32)
    p.add("vector", lambda v: v.memset(t[:], value), w=[name], tag=f"memset:{name}")
    return t


def finish(p, keys):
    p.add("sync", lambda s: s.nop(), r=list(keys), tag="finish")



QSL = 5632 // 8


def _slab_tiles(nrows):
    out = []
    r = 0
    while r < nrows:
        n = min(128, nrows - r)
        out.append((r, n))
        r += n
    return out


def build_prep_a(nc):
    p = Prog(nc)
    w1s = nc.dram_tensor("w1s", [QSL, 2048], F32, kind="ExternalInput").ap()
    w2s = nc.dram_tensor("w2s", [QSL, 2048], F32, kind="ExternalInput").ap()
    w3s = nc.dram_tensor("w3s", [2048, QSL], F32, kind="ExternalInput").ap()
    out = nc.dram_tensor("partials", [1, 3], F32, kind="ExternalOutput").ap()

    tl = p.sbuf("tl", [128, 2048], F32)
    cols = p.sbuf("cols", [128, 40], F32)
    onescol = bcast_const_col(p, "ones", 1.0)
    colsum = p.sbuf("colsum", [128, 1], F32)
    acc = p.psum("acc", [1, 3], F32)
    res = p.sbuf("res", [1, 3], F32)
    ncols = [0]

    p.add("vector", lambda v: v.memset(cols[:], 0.0), w=["cols"])

    def do_mat(j, src, tiles, width):
        for i, (r0, nr) in enumerate(tiles):
            ci = ncols[0]
            p.add("sync", lambda s, r0=r0, nr=nr, src=src, width=width:
                  s.dma_start(tl[0:nr, 0:width], src[r0:r0 + nr, :]),
                  w=["tl"], dma=True, tag=f"ld{j}_{i}")
            p.add("vector", lambda v, nr=nr, ci=ci, width=width:
                  v.reduce_sum(cols[0:nr, ci:ci + 1], tl[0:nr, 0:width],
                               axis=mybir.AxisListType.X, apply_absolute_value=True),
                  r=["tl"], w=["cols"], tag=f"abs{j}_{i}")
            ncols[0] += 1

    mats = [(0, w1s, _slab_tiles(QSL), 2048), (1, w2s, _slab_tiles(QSL), 2048),
            (2, w3s, _slab_tiles(2048), QSL)]
    bounds = []
    for j, src, tiles, width in mats:
        c0 = ncols[0]
        do_mat(j, src, tiles, width)
        bounds.append((c0, ncols[0]))

    for j, (c0, c1) in enumerate(bounds):
        p.add("vector", lambda v, c0=c0, c1=c1:
              v.reduce_sum(colsum[:], cols[:, c0:c1], axis=mybir.AxisListType.X),
              r=["cols"], w=["colsum"], tag=f"sumcols{j}")
        p.add("tensor", lambda t, j=j:
              t.matmul(acc[0:1, j:j + 1], colsum[:], onescol[:], start=True, stop=True),
              r=["colsum", "ones"], w=["acc"], tag=f"psum{j}")
    p.add("scalar", lambda s: s.copy(res[:], acc[0:1, :]), r=["acc"], w=["res"])
    p.add("sync", lambda s: s.dma_start(out, res[:]), r=["res"], w=["out"], dma=True)
    finish(p, ["out"])
    p.emit_all()


def build_prep_b(nc):
    p = Prog(nc)
    w1s = nc.dram_tensor("w1s", [QSL, 2048], F32, kind="ExternalInput").ap()
    w2s = nc.dram_tensor("w2s", [QSL, 2048], F32, kind="ExternalInput").ap()
    w3s = nc.dram_tensor("w3s", [2048, QSL], F32, kind="ExternalInput").ap()
    partials = nc.dram_tensor("partials", [1, 24], F32, kind="ExternalInput").ap()
    q1t = nc.dram_tensor("q1t", [2048, QSL], BF16, kind="ExternalOutput").ap()
    q2t = nc.dram_tensor("q2t", [2048, QSL], BF16, kind="ExternalOutput").ap()
    q3t = nc.dram_tensor("q3t", [QSL, 2048], BF16, kind="ExternalOutput").ap()

    spart = p.sbuf("spart", [1, 24], F32)
    tot = p.sbuf("tot", [1, 3], F32)
    inv = p.sbuf("inv", [1, 3], F32)
    invb = p.sbuf("invb", [128, 3], F32)
    onesrow = p.sbuf("onesrow", [1, 128], F32)
    pinvb = p.psum("pinvb", [128, 3], F32)
    ident = p.sbuf("ident", [128, 128], BF16)
    identf = p.sbuf("identf", [128, 128], F32)

    p.add("sync", lambda s: s.dma_start(spart[:], partials), w=["spart"], dma=True)
    p.add("vector", lambda v: v.memset(onesrow[:], 1.0), w=["onesrow"])
    p.add("gpsimd", lambda g: g.iota(identf[:], [[-1, 128]], base=0, channel_multiplier=1,
                                     allow_small_or_imprecise_dtypes=True), w=["identf"])
    p.add("vector", lambda v: v.tensor_single_scalar(ident[:], identf[:], 0.0, ALU.is_equal),
          r=["identf"], w=["ident"])
    for j in range(3):
        p.add("vector", lambda v, j=j: v.reduce_sum(tot[0:1, j:j + 1], spart[0:1, 8 * j:8 * j + 8],
                                                    axis=mybir.AxisListType.X),
              r=["spart"], w=["tot"])
    p.add("vector", lambda v: v.tensor_scalar(tot[:], tot[:], 1.0 / (5632 * 2048), 1e-8,
                                              ALU.mult, ALU.add), r=["tot"], w=["tot"])
    p.add("vector", lambda v: v.reciprocal(inv[:], tot[:]), r=["tot"], w=["inv"])
    for j in range(3):
        p.add("tensor", lambda t, j=j:
              t.matmul(pinvb[:, j:j + 1], onesrow[0:1, :], inv[0:1, j:j + 1],
                       start=True, stop=(j == 2)),
              r=["onesrow", "inv"], w=["pinvb"], tag=f"bc{j}")
    p.add("scalar", lambda s: s.copy(invb[:], pinvb[:]), r=["pinvb"], w=["invb"])

    wt = p.sbuf("wt", [128, 2048], F32)
    z = p.sbuf("z", [128, 2048], F32)
    qp = p.sbuf("qp", [128, 2048], F32)
    q = p.sbuf("q", [128, 2048], BF16)
    ptr = [p.psum(f"ptr{i}", [128, 512], BF16) for i in range(2)]
    qt = p.sbuf("qt", [128, 512], BF16)

    def quant_mat(j, src, dst, tiles, width, name):
        for i, (r0, nr) in enumerate(tiles):
            tg = f"{name}_{i}"
            p.add("sync", lambda s, r0=r0, nr=nr: s.dma_start(wt[0:nr, 0:width], src[r0:r0 + nr, :]),
                  w=["wt"], dma=True, tag="ld" + tg)
            p.add("vector", lambda v, nr=nr: v.tensor_scalar(
                z[0:nr, 0:width], wt[0:nr, 0:width], invb[0:nr, j:j + 1], None, ALU.mult),
                r=["wt", "invb"], w=["z"], tag="z" + tg)
            p.add("vector", lambda v, nr=nr: v.tensor_single_scalar(
                qp[0:nr, 0:width], z[0:nr, 0:width], 0.5, ALU.is_gt),
                r=["z"], w=["qp"], tag="qp" + tg)
            p.add("vector", lambda v, nr=nr: v.tensor_single_scalar(
                z[0:nr, 0:width], z[0:nr, 0:width], -0.5, ALU.is_lt),
                r=["z"], w=["z"], tag="qn" + tg)
            p.add("vector", lambda v, nr=nr: v.tensor_sub(
                q[0:nr, 0:width], qp[0:nr, 0:width], z[0:nr, 0:width]),
                r=["qp", "z"], w=["q"], tag="q" + tg)
            for k in range(0, width, 128):
                kk = k // 128
                nk = min(128, width - k)
                pt = ptr[kk % 2]
                p.add("tensor", lambda t, k=k, nr=nr, pt=pt, nk=nk:
                      t.transpose(pt[0:nk, 0:nr], q[0:nr, k:k + nk], ident[0:nr, 0:nr]),
                      r=["q", "ident"], w=[pt.name], tag=f"tp{tg}_{kk}")
                p.add("scalar", lambda s, pt=pt, nr=nr, nk=nk: s.copy(qt[0:nk, 0:nr], pt[0:nk, 0:nr]),
                      r=[pt.name], w=["qt"], tag=f"ev{tg}_{kk}")
                p.add("sync", lambda s, k=k, r0=r0, nr=nr, nk=nk:
                      s.dma_start(dst[k:k + nk, r0:r0 + nr], qt[0:nk, 0:nr]),
                      r=["qt"], w=[f"dst{name}"], dma=True, tag=f"st{tg}_{kk}")

    quant_mat(0, w1s, q1t, _slab_tiles(QSL), 2048, "q1")
    quant_mat(1, w2s, q2t, _slab_tiles(QSL), 2048, "q2")
    quant_mat(2, w3s, q3t, _slab_tiles(2048), QSL, "q3")
    finish(p, ["dstq1", "dstq2", "dstq3"])
    p.emit_all()




def ln_tokmajor(p, x, xk, out, outk, negmean, rstd, sq_scratch, ssq, D, eps, tagp,
                nt=128, scale_out=1.0):
    nm, sq, rs, scr = "k_nm", "k_ssq", "k_rstd", "k_sqscr"
    p.add("vector", lambda v: v.reduce_sum(negmean[0:nt], x, axis=mybir.AxisListType.X,
                                           negate=True), r=[xk], w=[nm], tag=nm)
    p.add("vector", lambda v: v.tensor_scalar_mul(negmean[0:nt], negmean[0:nt], 1.0 / D),
          r=[nm], w=[nm], tag=nm + "2")
    p.add("scalar", lambda s: s.activation(sq_scratch[0:nt], x, AF.Square,
                                           accum_out=ssq[0:nt]),
          r=[xk], w=[sq, scr], tag=sq)
    p.add("vector", lambda v: v.tensor_scalar_mul(ssq[0:nt], ssq[0:nt], 1.0 / D),
          r=[sq], w=[sq], tag=sq + "2")
    p.add("vector", lambda v: v.scalar_tensor_tensor(rstd[0:nt], negmean[0:nt], 1.0,
                                                     negmean[0:nt], ALU.mult, ALU.mult),
          r=[nm], w=[rs], tag=rs + "m2")
    p.add("vector", lambda v: v.tensor_sub(ssq[0:nt], ssq[0:nt], rstd[0:nt]),
          r=[sq, rs], w=[sq], tag=sq + "var")
    epst = _eps_col(p, eps)
    p.add("scalar", lambda s: s.activation(rstd[0:nt], ssq[0:nt], AF.Sqrt,
                                           bias=epst[0:nt]),
          r=[sq, f"epscol{eps}"], w=[rs], tag=rs + "sd")
    p.add("vector", lambda v: v.reciprocal(rstd[0:nt], rstd[0:nt]), r=[rs], w=[rs],
          tag=rs + "rec")
    if scale_out != 1.0:
        p.add("vector", lambda v: v.tensor_scalar_mul(rstd[0:nt], rstd[0:nt], scale_out),
              r=[rs], w=[rs], tag=rs + "sc")
    p.add("vector", lambda v: v.tensor_scalar(out, x, negmean[0:nt], rstd[0:nt],
                                              ALU.add, ALU.mult),
          r=[xk, nm, rs], w=[outk], tag=tagp + "apply")


_EPS_COLS = {}


def _eps_col(p, eps):
    key = f"epscol{eps}"
    if key not in _EPS_COLS or _EPS_COLS[key][0] is not p:
        t = p.sbuf(key, [128, 1], F32)
        p.add("vector", lambda v: v.memset(t[:], eps), w=[key])
        _EPS_COLS[key] = (p, t)
    return _EPS_COLS[key][1]


def add_identity(p, nc):
    identf = p.sbuf("identf", [128, 128], F32)
    ident = p.sbuf("ident", [128, 128], BF16)
    p.add("gpsimd", lambda g: g.iota(identf[:], [[-1, 128]], base=0, channel_multiplier=1,
                                     allow_small_or_imprecise_dtypes=True), w=["identf"])
    p.add("vector", lambda v: v.tensor_single_scalar(ident[:], identf[:], 0.0, ALU.is_equal),
          r=["identf"], w=["ident"])
    p.add("vector", lambda v: v.tensor_single_scalar(identf[:], identf[:], 0.0, ALU.is_equal),
          r=["identf"], w=["identf"])
    return ident, identf



NTOK = 2048
HBLK = 256


def build_ffn0(nc):
    p = Prog(nc)
    x1 = nc.dram_tensor("x1", [NTOK, 2048], F32, kind="ExternalInput").ap()
    q1t = nc.dram_tensor("q1t", [2048, 5632], BF16, kind="ExternalInput").ap()
    q2t = nc.dram_tensor("q2t", [2048, 5632], BF16, kind="ExternalInput").ap()
    q3t = nc.dram_tensor("q3t", [5632, 2048], BF16, kind="ExternalInput").ap()
    x2 = nc.dram_tensor("x2", [NTOK, 2048], F32, kind="ExternalOutput").ap()

    ident, identf = add_identity(p, nc)
    xtok = p.sbuf("xtok", [128, 2048], F32)
    xln = p.sbuf("xln", [128, 2048], F32)
    sq = p.sbuf("sqscr", [128, 2048], BF16)
    nm = p.sbuf("nm", [128, 1], F32)
    rs = p.sbuf("rs", [128, 1], F32)
    ssq = p.sbuf("ssq", [128, 1], F32)
    xlnT = p.sbuf("xlnT", [128, 16 * 512], BF16)
    q12w = p.sbuf("q12w", [128, 2 * 16 * HBLK], BF16)
    hid = p.sbuf("hid", [128, 44 * 512], BF16)
    sgd = p.sbuf("sgd", [128, 512], BF16)
    q3w = p.sbuf("q3w", [128, 44 * 512], BF16)
    x3tok = p.sbuf("x3tok", [128, 4 * 2048], BF16)
    lnout = p.sbuf("lnout", [128, 2048], F32)
    banks = [p.psum(f"bk{i}", [128, 512], F32) for i in range(8)]

    def ldw2(dst, dstk, src, h0, w, tag):
        ap = bass.AP(src.tensor, h0, [[5632, 128], [128 * 5632, 16], [1, w]])
        dst3 = dst.rearrange("p (a b) -> p a b", b=w)
        p.add("sync", lambda s: s.dma_start(dst3, ap), w=[dstk], dma=True, tag=tag)

    for tb in range(NTOK // 512):
        for sub in range(4):
            t0 = tb * 512 + sub * 128
            p.add("sync", lambda s, t0=t0: s.dma_start(xtok[:], x1[t0:t0 + 128, :]),
                  w=["xtok"], dma=True, tag=f"ldx{tb}_{sub}")
            ln_tokmajor(p, xtok[:], "xtok", xln[:], "xln", nm, rs, sq, ssq,
                        2048, 1e-5, f"ln{tb}_{sub}")
            for dt in range(16):
                bk = banks[dt % 2]
                p.add("tensor", lambda t, dt=dt, bk=bk:
                      t.transpose(bk[:, 0:128], xln[:, dt * 128:(dt + 1) * 128], identf[:]),
                      r=["xln", "identf"], w=[bk.name], tag=f"tp{tb}_{sub}_{dt}")
                p.add("scalar", lambda s, dt=dt, sub=sub, bk=bk:
                      s.copy(xlnT[:, dt * 512 + sub * 128: dt * 512 + sub * 128 + 128],
                             bk[:, 0:128]),
                      r=[bk.name], w=[f"xlnT{dt}"], tag=f"ev{tb}_{sub}_{dt}")
        for hb in range(5632 // HBLK):
            h0 = hb * HBLK
            ldw2(q12w[:, 0:16 * HBLK], "q12w", q1t, h0, HBLK, f"lq1_{tb}_{hb}")
            ldw2(q12w[:, 16 * HBLK:], "q12w", q2t, h0, HBLK, f"lq2_{tb}_{hb}")
            for m in range(2):
                for f in range(HBLK // 128):
                    bk = banks[4 * (hb % 2) + 2 * m + f]
                    for dt in range(16):
                        p.add("tensor", lambda t, m=m, f=f, dt=dt, bk=bk:
                              t.matmul(bk[:], q12w[:, m * 16 * HBLK + dt * HBLK + f * 128:
                                               m * 16 * HBLK + dt * HBLK + f * 128 + 128],
                                       xlnT[:, dt * 512:(dt + 1) * 512],
                                       start=(dt == 0), stop=(dt == 15)),
                              r=["q12w"] + [f"xlnT{dt}"], w=[bk.name],
                              tag=f"mm{tb}_{hb}_{m}_{f}_{dt}")
            for f in range(HBLK // 128):
                b1 = banks[4 * (hb % 2) + f]
                b2 = banks[4 * (hb % 2) + 2 + f]
                hcol = (h0 + f * 128) // 128 * 512
                p.add("scalar", lambda s, b1=b1: s.activation(sgd[:], b1[:], AF.Sigmoid),
                      r=[b1.name], w=["sgd"], tag=f"sg{tb}_{hb}_{f}")
                p.add("vector", lambda v, b2=b2, hcol=hcol:
                      v.tensor_mul(hid[:, hcol:hcol + 512], sgd[:], b2[:]),
                      r=["sgd", b2.name], w=[f"hid{hcol // 512}"], tag=f"hm{tb}_{hb}_{f}")
        for db in range(4):
            ap3 = bass.AP(q3t.tensor, db * 512, [[2048, 128], [128 * 2048, 44], [1, 512]])
            p.add("sync", lambda s, ap3=ap3: s.dma_start(
                      q3w[:].rearrange("p (a b) -> p a b", b=512), ap3),
                  w=["q3w"], dma=True, tag=f"lq3_{tb}_{db}")
            for sub in range(4):
                bk = banks[(db * 4 + sub) % 8]
                for ht in range(44):
                    p.add("tensor", lambda t, ht=ht, sub=sub, bk=bk:
                          t.matmul(bk[:], hid[:, ht * 512 + sub * 128: ht * 512 + sub * 128 + 128],
                                   q3w[:, ht * 512:(ht + 1) * 512],
                                   start=(ht == 0), stop=(ht == 43)),
                          r=[f"hid{ht}", "q3w"], w=[bk.name],
                          tag=f"m3{tb}_{db}_{sub}_{ht}")
                p.add("scalar", lambda s, sub=sub, db=db, bk=bk:
                      s.copy(x3tok[:, sub * 2048 + db * 512: sub * 2048 + (db + 1) * 512], bk[:]),
                      r=[bk.name], w=[f"x3t{sub}_{db}"], tag=f"e3{tb}_{db}_{sub}")
        for sub in range(4):
            t0 = tb * 512 + sub * 128
            xv = x3tok[:, sub * 2048:(sub + 1) * 2048]
            xk = [f"x3t{sub}_{db}" for db in range(4)]
            nmk, rsk = f"lnb{tb}_{sub}nm", f"lnb{tb}_{sub}rstd"
            ln_tokmajor2(p, xv, xk, lnout[:], "lnout", nm, rs, sq, ssq,
                         2048, 1e-5, f"lnb{tb}_{sub}")
            p.add("sync", lambda s, t0=t0: s.dma_start(xtok[:], x1[t0:t0 + 128, :]),
                  w=["xtok"], dma=True, tag=f"ldr{tb}_{sub}")
            p.add("vector", lambda v: v.tensor_add(lnout[:], lnout[:], xtok[:]),
                  r=["lnout", "xtok"], w=["lnout"], tag=f"add{tb}_{sub}")
            p.add("sync", lambda s, t0=t0: s.dma_start(x2[t0:t0 + 128, :], lnout[:]),
                  r=["lnout"], w=["x2out"], dma=True, tag=f"st{tb}_{sub}")
    finish(p, ["x2out"])
    p.emit_all()


def ln_tokmajor2(p, x, xkeys, out, outk, negmean, rstd, sq_scratch, ssq, D, eps, tagp,
                 nt=128, scale_out=1.0):
    nm, sq, rs, scr = "k_nm", "k_ssq", "k_rstd", "k_sqscr"
    p.add("vector", lambda v: v.reduce_sum(negmean[0:nt], x, axis=mybir.AxisListType.X,
                                           negate=True), r=list(xkeys), w=[nm], tag=nm)
    p.add("vector", lambda v: v.tensor_scalar_mul(negmean[0:nt], negmean[0:nt], 1.0 / D),
          r=[nm], w=[nm], tag=nm + "2")
    p.add("scalar", lambda s: s.activation(sq_scratch[0:nt], x, AF.Square,
                                           accum_out=ssq[0:nt]),
          r=list(xkeys), w=[sq, scr], tag=sq)
    p.add("vector", lambda v: v.tensor_scalar_mul(ssq[0:nt], ssq[0:nt], 1.0 / D),
          r=[sq], w=[sq], tag=sq + "2")
    p.add("vector", lambda v: v.scalar_tensor_tensor(rstd[0:nt], negmean[0:nt], 1.0,
                                                     negmean[0:nt], ALU.mult, ALU.mult),
          r=[nm], w=[rs], tag=rs + "m2")
    p.add("vector", lambda v: v.tensor_sub(ssq[0:nt], ssq[0:nt], rstd[0:nt]),
          r=[sq, rs], w=[sq], tag=sq + "var")
    epst = _eps_col(p, eps)
    p.add("scalar", lambda s: s.activation(rstd[0:nt], ssq[0:nt], AF.Sqrt,
                                           bias=epst[0:nt]),
          r=[sq, f"epscol{eps}"], w=[rs], tag=rs + "sd")
    p.add("vector", lambda v: v.reciprocal(rstd[0:nt], rstd[0:nt]), r=[rs], w=[rs],
          tag=rs + "rec")
    if scale_out != 1.0:
        p.add("vector", lambda v: v.tensor_scalar_mul(rstd[0:nt], rstd[0:nt], scale_out),
              r=[rs], w=[rs], tag=rs + "sc")
    p.add("vector", lambda v: v.tensor_scalar(out, x, negmean[0:nt], rstd[0:nt],
                                              ALU.add, ALU.mult),
          r=list(xkeys) + [nm, rs], w=[outk], tag=tagp + "apply")



ROPE_HALF = 32
LN2_10000 = float(np.log(10000.0))
TWO_PI = float(2 * np.pi)
PI = float(np.pi)


def rope_tables(p, pos, tb, tagbase):
    if tb == 0:
        p._invf2 = p.sbuf("invf2", [1, 64], F32)
        p._angp = p.psum("angp", [128, 512], F32)
        p._angs = p.sbuf("angs", [64, 512], F32)
        p._angk = p.sbuf("angk", [64, 512], I32)
        p._angks = p.sbuf("angks", [64, 512], F32)
        iot = p.sbuf("iotr", [1, 64], F32)
        p.add("gpsimd", lambda g: g.iota(iot[0:1, :].rearrange("p (a b) -> p a b", b=32),
                                         [[0, 2], [1, 32]], base=0, channel_multiplier=0,
                                         allow_small_or_imprecise_dtypes=True), w=["iotr"])
        p.add("scalar", lambda s: s.activation(p._invf2[:], iot[:], AF.Exp,
                                               scale=-LN2_10000 / 32.0),
              r=["iotr"], w=["invf2"])
    invf2, ang, angs, kk, ks = p._invf2, p._angp, p._angs, p._angk, p._angks
    sins = p.sbuf(f"sins{tb}", [64, 512], F32)
    coss = p.sbuf(f"coss{tb}", [64, 512], F32)

    def reduce_and_sin(dst, dstk, shift, tg):
        if shift == 0.0:
            p.add("scalar", lambda s: s.copy(angs[:], ang[0:64, :]), r=["angp"], w=["angs"], tag=tg + "c")
        else:
            sh_t = _eps_col(p, shift)
            p.add("scalar", lambda s: s.activation(angs[:], ang[0:64, :], AF.Identity,
                                                   bias=sh_t[0:64]),
                  r=["angp", f"epscol{shift}"], w=["angs"], tag=tg + "c")
        p.add("vector", lambda v: v.tensor_scalar_mul(ks[:], angs[:], 1.0 / TWO_PI),
              r=["angs"], w=["angks"], tag=tg + "r0")
        p.add("vector", lambda v: v.tensor_copy(kk[:], ks[:]), r=["angks"], w=["angk"], tag=tg + "r1")
        p.add("vector", lambda v: v.tensor_copy(ks[:], kk[:]), r=["angk"], w=["angks"], tag=tg + "r2")
        p.add("vector", lambda v: v.scalar_tensor_tensor(angs[:], ks[:], -TWO_PI, angs[:],
                                                         ALU.mult, ALU.add),
              r=["angks", "angs"], w=["angs"], tag=tg + "r3")
        p.add("vector", lambda v: v.tensor_single_scalar(ks[:], angs[:], PI, ALU.is_gt),
              r=["angs"], w=["angks"], tag=tg + "r4")
        p.add("vector", lambda v: v.scalar_tensor_tensor(angs[:], ks[:], -TWO_PI, angs[:],
                                                         ALU.mult, ALU.add),
              r=["angks", "angs"], w=["angs"], tag=tg + "r5")
        p.add("vector", lambda v: v.tensor_single_scalar(ks[:], angs[:], -PI, ALU.is_lt),
              r=["angs"], w=["angks"], tag=tg + "r6")
        p.add("vector", lambda v: v.scalar_tensor_tensor(angs[:], ks[:], TWO_PI, angs[:],
                                                         ALU.mult, ALU.add),
              r=["angks", "angs"], w=["angs"], tag=tg + "r7")
        p.add("scalar", lambda s: s.activation(dst[:], angs[:], AF.Sin),
              r=["angs"], w=[dstk], tag=tg + "sin")

    p.add("tensor", lambda t: t.matmul(ang[0:64, :], invf2[:], pos[0:1, tb * 512:(tb + 1) * 512],
                                       start=True, stop=True),
          r=["invf2", "pos"], w=["angp"], tag=f"{tagbase}angmm")
    reduce_and_sin(sins, f"sins{tb}", 0.0, f"{tagbase}s")
    reduce_and_sin(coss, f"coss{tb}", PI / 2, f"{tagbase}q")
    return sins, coss, f"sins{tb}", f"coss{tb}"


def add_psign(p):
    ps = p.sbuf("psign", [64, 64], F32)
    t1 = p.sbuf("psign_t", [64, 64], F32)
    p.add("gpsimd", lambda g: g.iota(t1[:], [[-1, 64]], base=0, channel_multiplier=1,
                                     allow_small_or_imprecise_dtypes=True), w=["psign_t"])
    p.add("vector", lambda v: v.tensor_single_scalar(ps[:], t1[:], -32.0, ALU.is_equal),
          r=["psign_t"], w=["psign"], tag="psg1")
    p.add("vector", lambda v: v.tensor_single_scalar(t1[:], t1[:], 32.0, ALU.is_equal),
          r=["psign_t"], w=["psign_t"], tag="psg2")
    p.add("vector", lambda v: v.tensor_sub(ps[:], ps[:], t1[:]),
          r=["psign", "psign_t"], w=["psign"], tag="psg3")
    return ps


def rope_apply(p, buf, bufk, sins, coss, sk, ck, tagp, nt=512):
    ps, pq, s1, s2 = p._psign, p._rpps, p._rsc1, p._rsc2
    p.add("tensor", lambda t: t.matmul(pq[0:64, 0:nt], ps[:], buf[0:64, 0:nt],
                                       start=True, stop=True),
          r=["psign", bufk], w=["rpps"], tag=tagp + "pq")
    p.add("vector", lambda v: v.tensor_mul(s1[:, 0:nt], buf[0:64, 0:nt], coss[:, 0:nt]),
          r=[bufk, ck], w=["rsc1"], tag=tagp + "c")
    p.add("vector", lambda v: v.tensor_mul(s2[:, 0:nt], pq[0:64, 0:nt], sins[:, 0:nt]),
          r=["rpps", sk], w=["rsc2"], tag=tagp + "s")
    p.add("vector", lambda v: v.tensor_add(buf[0:64, 0:nt], s1[:, 0:nt], s2[:, 0:nt]),
          r=["rsc1", "rsc2"], w=[bufk], tag=tagp + "rot")


def build_l1proj(nc):
    p = Prog(nc)
    x = nc.dram_tensor("x", [NTOK, 2048], F32, kind="ExternalInput").ap()
    wqa = nc.dram_tensor("wqa", [2048, 1536], F32, kind="ExternalInput").ap()
    wqbp = nc.dram_tensor("wqbp", [1536, 2048], F32, kind="ExternalInput").ap()
    wkvap = nc.dram_tensor("wkvap", [2048, 576], F32, kind="ExternalInput").ap()
    wkvbk = nc.dram_tensor("wkvbk", [512, 1024], F32, kind="ExternalInput").ap()
    wkvbv = nc.dram_tensor("wkvbv", [512, 2048], F32, kind="ExternalInput").ap()
    pos_t = nc.dram_tensor("pos", [1, NTOK], F32, kind="ExternalInput").ap()
    qt = nc.dram_tensor("qt", [2048, NTOK], BF16, kind="ExternalOutput").ap()
    kt = nc.dram_tensor("kt", [2048, NTOK], BF16, kind="ExternalOutput").ap()
    vv = nc.dram_tensor("vv", [NTOK, 2048], BF16, kind="ExternalOutput").ap()

    ident, identf = add_identity(p, nc)
    p._identf = identf
    p._psign = add_psign(p)
    p._rsc1 = p.sbuf("rsc1", [64, 512], F32)
    p._rsc2 = p.sbuf("rsc2", [64, 512], F32)
    xtok = p.sbuf("xtok", [128, 2048], F32)
    xln = p.sbuf("xln", [128, 2048], F32)
    sq = p.sbuf("sqscr", [128, 2048], BF16)
    nm = p.sbuf("nm", [128, 1], F32)
    rs = p.sbuf("rs", [128, 1], F32)
    ssq = p.sbuf("ssq", [128, 1], F32)
    xlnT = p.sbuf("xlnT", [128, 16 * 512], BF16)
    pos_s = p.sbuf("pos_s", [1, NTOK], F32)
    w1 = p.sbuf("w1buf", [128, 16 * 128], BF16)
    w1b = p.sbuf("w1bufb", [128, 16 * 128], BF16)
    qlat = p.sbuf("qlat", [128, 12 * 512], BF16)
    ckv = p.sbuf("ckv", [128, 4 * 512], BF16)
    hbuf = p.sbuf("hbuf", [128, 512], F32)
    obuf = p.sbuf("obuf", [128, 512], BF16)
    banks = [p.psum(f"bk{i}", [128, 512], F32) for i in range(6)]
    rpps = p.psum("rpps", [128, 512], F32)
    p._rpps = rpps
    p.add("sync", lambda s: s.dma_start(pos_s[:], pos_t), w=["pos"], dma=True)

    def ldw(dst, dstk, src_t, srck, r0, c0, ncon, nf, width, tag):
        ap = bass.AP(src_t.tensor, r0 * width + c0,
                     [[width, 128], [128 * width, ncon], [1, nf]])
        dst3 = dst[:, 0:ncon * nf].rearrange("p (a b) -> p a b", b=nf)
        p.add("gpsimd", lambda g: g.dma_start(dst3, ap), w=[dstk], dma=True, tag=tag)

    for tb in range(NTOK // 512):
        for sub in range(4):
            t0 = tb * 512 + sub * 128
            p.add("sync", lambda s, t0=t0: s.dma_start(xtok[:], x[t0:t0 + 128, :]),
                  w=["xtok"], dma=True, tag=f"ldx{tb}_{sub}")
            ln_tokmajor(p, xtok[:], "xtok", xln[:], "xln", nm, rs, sq, ssq,
                        2048, 1e-5, f"ln{tb}_{sub}")
            for dt in range(16):
                bk = banks[dt % 2]
                p.add("tensor", lambda t, dt=dt, bk=bk:
                      t.transpose(bk[:, 0:128], xln[:, dt * 128:(dt + 1) * 128],
                                  p._identf[:]),
                      r=["xln", "identf"], w=[bk.name], tag=f"tp{tb}_{sub}_{dt}")
                p.add("scalar", lambda s, dt=dt, sub=sub, bk=bk:
                      s.copy(xlnT[:, dt * 512 + sub * 128: dt * 512 + sub * 128 + 128],
                             bk[:, 0:128]),
                      r=[bk.name], w=[f"xlnT{dt}"], tag=f"ev{tb}_{sub}_{dt}")
        sins, coss, sk, ck = rope_tables(p, pos_s, tb, f"rt{tb}")
        for f in range(12):
            bk = banks[2 + f % 2]
            ldw(w1[:], "w1buf", wqa, "wqa", 0, f * 128, 16, 128, 1536, f"lwqa{tb}_{f}")
            for dt in range(16):
                p.add("tensor", lambda t, dt=dt, bk=bk:
                      t.matmul(bk[:], w1[:, dt * 128:(dt + 1) * 128],
                               xlnT[:, dt * 512:(dt + 1) * 512],
                               start=(dt == 0), stop=(dt == 15)),
                      r=["w1buf"] + [f"xlnT{dt}"], w=[bk.name], tag=f"mqa{tb}_{f}_{dt}")
            p.add("scalar", lambda s, f=f, bk=bk:
                  s.copy(qlat[:, f * 512:(f + 1) * 512], bk[:]),
                  r=[bk.name], w=[f"qlat{f}"], tag=f"eqa{tb}_{f}")
        for f in range(5):
            nf = 128 if f < 4 else 64
            bk = banks[4 + f % 2]
            ldw(w1b[:], "w1bufb", wkvap, "wkvap", 0, f * 128, 16, nf, 576, f"lkva{tb}_{f}")
            for dt in range(16):
                p.add("tensor", lambda t, dt=dt, bk=bk, nf=nf:
                      t.matmul(bk[0:nf, :],
                               w1b[:, dt * nf:(dt + 1) * nf],
                               xlnT[:, dt * 512:(dt + 1) * 512],
                               start=(dt == 0), stop=(dt == 15)),
                      r=["w1bufb"] + [f"xlnT{dt}"], w=[bk.name], tag=f"mkva{tb}_{f}_{dt}")
            if f < 4:
                p.add("scalar", lambda s, f=f, bk=bk:
                      s.copy(ckv[:, f * 512:(f + 1) * 512], bk[:]),
                      r=[bk.name], w=[f"ckv{f}"], tag=f"ekva{tb}_{f}")
            else:
                p.add("scalar", lambda s, bk=bk: s.copy(hbuf[0:64, :], bk[0:64, :]),
                      r=[bk.name], w=["hbuf"], tag=f"ekpe{tb}")
                rope_apply(p, hbuf, "hbuf", sins, coss, sk, ck, f"rkpe{tb}")
                p.add("vector", lambda v: v.tensor_copy(obuf[0:64, :], hbuf[0:64, :]),
                      r=["hbuf"], w=["obuf"], tag=f"ckpe{tb}")
                for h in range(16):
                    p.add("sync", lambda s, h=h, tb=tb:
                          s.dma_start(kt[h * 128:h * 128 + 64, tb * 512:(tb + 1) * 512],
                                      obuf[0:64, :]),
                          r=["obuf"], w=["ktout"], dma=True, tag=f"skpe{tb}_{h}")
        build_l1proj_tail(p, tb, (sins, coss, sk, ck), wqbp, wkvbk, wkvbv, qt, kt, vv,
                          banks, w1, w1b, qlat, ckv, hbuf, obuf)
    finish(p, ["qtout", "ktout", "vvout"])
    p.emit_all()


def build_l1proj_tail(p, tb, cstuple, wqbp, wkvbk, wkvbv, qt, kt, vv, banks,
                      w1, w1b, qlat, ckv, hbuf, obuf):
    sins, coss, sk, ck = cstuple
    QS = float(128.0 ** -0.5)

    def ldw(dst, dstk, src_t, r0, c0, ncon, nf, width, tag):
        ap = bass.AP(src_t.tensor, r0 * width + c0,
                     [[width, 128], [128 * width, ncon], [1, nf]])
        dst3 = dst[:, 0:ncon * nf].rearrange("p (a b) -> p a b", b=nf)
        p.add("gpsimd", lambda g: g.dma_start(dst3, ap), w=[dstk], dma=True, tag=tag)

    for h in range(16):
        bk = banks[h % 2]
        ldw(w1, "w1buf", wqbp, 0, h * 128, 12, 128, 2048, f"lwqb{tb}_{h}")
        for ct in range(12):
            p.add("tensor", lambda t, ct=ct, bk=bk:
                  t.matmul(bk[:], w1[:, ct * 128:(ct + 1) * 128],
                           qlat[:, ct * 512:(ct + 1) * 512],
                           start=(ct == 0), stop=(ct == 11)),
                  r=["w1buf", f"qlat{ct}"], w=[bk.name], tag=f"mq{tb}_{h}_{ct}")
        p.add("scalar", lambda s, bk=bk: s.copy(hbuf[:], bk[:]),
              r=[bk.name], w=["hbuf"], tag=f"eq{tb}_{h}")
        rope_apply(p, hbuf, "hbuf", sins, coss, sk, ck, f"rq{tb}_{h}")
        p.add("scalar", lambda s: s.activation(obuf[:], hbuf[:], AF.Identity, scale=QS),
              r=["hbuf"], w=["obuf"], tag=f"cq{tb}_{h}")
        p.add("sync", lambda s, h=h: s.dma_start(qt[h * 128:(h + 1) * 128,
                                                    tb * 512:(tb + 1) * 512], obuf[:]),
              r=["obuf"], w=["qtout"], dma=True, tag=f"sq{tb}_{h}")
    for f in range(8):
        bk = banks[2 + f % 2]
        ldw(w1b, "w1bufb", wkvbk, 0, f * 128, 4, 128, 1024, f"lwkk{tb}_{f}")
        for ct in range(4):
            p.add("tensor", lambda t, ct=ct, bk=bk:
                  t.matmul(bk[:], w1b[:, ct * 128:(ct + 1) * 128],
                           ckv[:, ct * 512:(ct + 1) * 512],
                           start=(ct == 0), stop=(ct == 3)),
                  r=["w1bufb", f"ckv{ct}"], w=[bk.name], tag=f"mk{tb}_{f}_{ct}")
        p.add("scalar", lambda s, bk=bk: s.copy(obuf[:], bk[:]),
              r=[bk.name], w=["obuf"], tag=f"ek{tb}_{f}")
        for hh in range(2):
            h = 2 * f + hh
            p.add("sync", lambda s, h=h, hh=hh:
                  s.dma_start(kt[h * 128 + 64:(h + 1) * 128, tb * 512:(tb + 1) * 512],
                              obuf[hh * 64:(hh + 1) * 64, :]),
                  r=["obuf"], w=["ktout"], dma=True, tag=f"sk{tb}_{f}_{hh}")
    for sub in range(4):
        for vb in range(4):
            bk = banks[4 + (sub * 4 + vb) % 2]
            ldw(w1b, "w1bufb", wkvbv, 0, vb * 512, 4, 512, 2048, f"lwv{tb}_{sub}_{vb}")
            for ct in range(4):
                p.add("tensor", lambda t, ct=ct, sub=sub, bk=bk:
                      t.matmul(bk[:], ckv[:, ct * 512 + sub * 128: ct * 512 + sub * 128 + 128],
                               w1b[:, ct * 512:(ct + 1) * 512],
                               start=(ct == 0), stop=(ct == 3)),
                      r=["w1bufb", f"ckv{ct}"], w=[bk.name], tag=f"mv{tb}_{sub}_{vb}_{ct}")
            p.add("scalar", lambda s, bk=bk: s.copy(obuf[:, 0:512], bk[:]),
                  r=[bk.name], w=["obuf"], tag=f"evv{tb}_{sub}_{vb}")
            t0 = tb * 512 + sub * 128
            p.add("sync", lambda s, t0=t0, vb=vb:
                  s.dma_start(vv[t0:t0 + 128, vb * 512:(vb + 1) * 512], obuf[:, 0:512]),
                  r=["obuf"], w=["vvout"], dma=True, tag=f"sv{tb}_{sub}_{vb}")



SLOTK = [2048, 4096, 6144, 8192]
SLOTO = [0, 2048, 6144, 12288]


def build_attn(nc):
    p = Prog(nc)
    qtq = nc.dram_tensor("qtq", [2048, 2048], BF16, kind="ExternalInput").ap()
    ktq = nc.dram_tensor("ktq", [2048, 20480], BF16, kind="ExternalInput").ap()
    vvq = nc.dram_tensor("vvq", [20480, 2048], BF16, kind="ExternalInput").ap()
    kvm = nc.dram_tensor("kvm", [128, 160], F32, kind="ExternalInput").ap()
    ao = nc.dram_tensor("ao", [2048, 2048], BF16, kind="ExternalOutput").ap()

    qsb = p.sbuf("qsb", [128, 16 * 4 * 512], BF16)
    ktb = p.sbuf("ktb", [128, 4 * 1024], BF16)
    vtb = p.sbuf("vtb", [128, 8 * 512], BF16)
    kvb = p.sbuf("kvb", [128, 160], F32)
    msrc = p.sbuf("msrc", [128, 512], F32)
    masks = [p.sbuf(f"mask{m}", [128, 512], F32) for m in range(4)]
    accs = [p.sbuf(f"acc{i}", [128, 512], F32) for i in range(4)]
    pbuf = p.sbuf("pbuf", [128, 512], BF16)
    onescol = p.sbuf("onescol", [128, 1], F32)
    onesrow = p.sbuf("onesrow", [1, 128], F32)
    recip = p.sbuf("recip", [1, 512], F32)
    att_s = p.sbuf("att_s", [128, 512], BF16)
    out_s = p.sbuf("out_s", [128, 512], BF16)
    at_bk = [p.psum(f"at{i}", [128, 512], F32) for i in range(4)]
    sc_bk = [p.psum(f"sc{i}", [128, 512], F32) for i in range(2)]
    mb = p.psum("mb", [128, 512], F32)

    p.add("vector", lambda v: v.memset(onescol[:], 1.0), w=["onescol"])
    p.add("vector", lambda v: v.memset(onesrow[:], 1.0), w=["onesrow"])
    p.add("gpsimd", lambda g: g.iota(msrc[:], [[1, 512]], base=0, channel_multiplier=-1,
                                     allow_small_or_imprecise_dtypes=True), w=["msrc"])
    for m in range(4):
        p.add("vector", lambda v, m=m: v.tensor_scalar(masks[m][:], msrc[:],
                                                       float(128 * m), -1e30,
                                                       ALU.is_lt, ALU.mult),
              r=["msrc"], w=[f"mask{m}"], tag=f"mk{m}")
    p.add("sync", lambda s: s.dma_start(kvb[:], kvm), w=["kvb"], dma=True)
    for s in range(4):
        ap = bass.AP(qtq.tensor, s * 512, [[2048, 128], [128 * 2048, 16], [1, 512]])
        p.add("sync", lambda sy, ap=ap, s=s: sy.dma_start(
            qsb[:, :].rearrange("p (h s2 j) -> p (h s2) j", h=16, j=512)[:, s::4, :], ap),
            w=["qsb"], dma=True, tag=f"lq{s}")

    pstore = p.sbuf("pstore", [128, 32 * 512], BF16)
    attacc = [p.sbuf(f"attacc{i}", [128, 512], F32) for i in range(4)]
    for s in range(4):
        ntile = SLOTK[s] // 128
        for hg in range(4):
            for i in range(4):
                p.add("vector", lambda v, i=i: v.memset(accs[i][:], 0.0),
                      w=[f"acc{i}"], tag=f"z{s}_{hg}_{i}")
                p.add("vector", lambda v, i=i: v.memset(attacc[i][:], 0.0),
                      w=[f"attacc{i}"], tag=f"za{s}_{hg}_{i}")
            nch = SLOTK[s] // 1024
            for ch in range(nch):
                k0 = SLOTO[s] + ch * 1024
                apk = bass.AP(ktq.tensor, (hg * 512) * 20480 + k0,
                              [[20480, 128], [128 * 20480, 4], [1, 1024]])
                p.add("sync", lambda sy, apk=apk: sy.dma_start(
                    ktb[:].rearrange("p (a b) -> p a b", b=1024), apk),
                    w=["ktb"], dma=True, tag=f"lk{s}_{hg}_{ch}")
                apv = bass.AP(vvq.tensor, k0 * 2048 + hg * 512,
                              [[2048, 128], [128 * 2048, 8], [1, 512]])
                p.add("sync", lambda sy, apv=apv: sy.dma_start(
                    vtb[:].rearrange("p (a b) -> p a b", b=512), apv),
                    w=["vtb"], dma=True, tag=f"lv{s}_{hg}_{ch}")
                for kt in range(8):
                    gt = ch * 8 + kt
                    kvcol = (SLOTO[s] // 128) + gt
                    dm = gt - (ntile - 4)
                    for h4 in range(4):
                        bk = sc_bk[(kt * 4 + h4) % 2]
                        pcol = (kt * 4 + h4) * 512
                        p.add("tensor", lambda t, h4=h4, kt=kt, bk=bk, hg=hg, s=s:
                              t.matmul(bk[:], ktb[:, h4 * 1024 + kt * 128:
                                               h4 * 1024 + kt * 128 + 128],
                                       qsb[:, (hg * 4 + h4) * 2048 + s * 512:
                                           (hg * 4 + h4) * 2048 + (s + 1) * 512],
                                       start=True, stop=True),
                              r=["ktb", "qsb"], w=[bk.name], tag=f"qk{s}_{hg}_{gt}_{h4}")
                        if 0 <= dm < 4:
                            p.add("vector", lambda v, bk=bk, dm=dm:
                                  v.tensor_add(bk[:], bk[:], masks[dm][:]),
                                  r=[bk.name, f"mask{dm}"], w=[bk.name],
                                  tag=f"ms{s}_{hg}_{gt}_{h4}")
                        p.add("scalar", lambda sc, bk=bk, kvcol=kvcol, pcol=pcol:
                              sc.activation(pstore[:, pcol:pcol + 512], bk[:], AF.Exp,
                                            bias=kvb[:, kvcol:kvcol + 1]),
                              r=[bk.name, "kvb"], w=[f"ps{kt}_{h4}"],
                              tag=f"ex{s}_{hg}_{gt}_{h4}")
                        p.add("vector", lambda v, h4=h4, pcol=pcol: v.tensor_add(
                            accs[h4][:], accs[h4][:], pstore[:, pcol:pcol + 512]),
                            r=[f"acc{h4}", f"ps{kt}_{h4}"], w=[f"acc{h4}"],
                            tag=f"ac{s}_{hg}_{gt}_{h4}")
                for h4 in range(4):
                    bk = at_bk[h4 % 2]
                    for kt in range(8):
                        pcol = (kt * 4 + h4) * 512
                        p.add("tensor", lambda t, kt=kt, h4=h4, bk=bk, pcol=pcol:
                              t.matmul(bk[:], vtb[:, kt * 512 + h4 * 128:
                                                   kt * 512 + (h4 + 1) * 128],
                                       pstore[:, pcol:pcol + 512],
                                       start=(kt == 0), stop=(kt == 7)),
                              r=["vtb", f"ps{kt}_{h4}"], w=[bk.name],
                              tag=f"pv{s}_{hg}_{ch}_{h4}_{kt}")
                    p.add("vector", lambda v, h4=h4, bk=bk: v.tensor_add(
                        attacc[h4][:], attacc[h4][:], bk[:]),
                        r=[f"attacc{h4}", bk.name], w=[f"attacc{h4}"],
                        tag=f"aa{s}_{hg}_{ch}_{h4}")
            for h4 in range(4):
                h = hg * 4 + h4
                p.add("tensor", lambda t, h4=h4: t.matmul(
                    mb[0:1, :], onescol[:], accs[h4][:], start=True, stop=True),
                    r=[f"acc{h4}", "onescol"], w=["mb"], tag=f"sm{s}_{hg}_{h4}")
                p.add("vector", lambda v: v.reciprocal(recip[:], mb[0:1, :]),
                      r=["mb"], w=["recip"], tag=f"rc{s}_{hg}_{h4}")
                p.add("tensor", lambda t: t.matmul(mb[:], onesrow[:], recip[:],
                                                   start=True, stop=True),
                      r=["recip", "onesrow"], w=["mb"], tag=f"bc{s}_{hg}_{h4}")
                p.add("vector", lambda v, h4=h4: v.tensor_mul(out_s[:], attacc[h4][:], mb[:]),
                      r=[f"attacc{h4}", "mb"], w=["out_s"], tag=f"no{s}_{hg}_{h4}")
                p.add("sync", lambda sy, h=h, s=s: sy.dma_start(
                    ao[h * 128:(h + 1) * 128, s * 512:(s + 1) * 512], out_s[:]),
                    r=["out_s"], w=["aoout"], dma=True, tag=f"sa{s}_{hg}_{h4}")
    finish(p, ["aoout"])
    p.emit_all()


def attn_host_prep(c, qtb, ktb, vvb):
    import ml_dtypes
    qtq = np.zeros((2048, 2048), dtype=ml_dtypes.bfloat16)
    ktq = np.zeros((2048, 20480), dtype=ml_dtypes.bfloat16)
    vvq = np.zeros((20480, 2048), dtype=ml_dtypes.bfloat16)
    kvm = np.zeros((128, 160), dtype=np.float32)
    qbs = []
    for s in range(4):
        qb = 4 * s + ((c % 4) + s) % 4
        qbs.append(qb)
        qtq[:, s * 512:(s + 1) * 512] = qtb[:, qb * 512:(qb + 1) * 512]
        o = SLOTO[s]
        nv = qb * 512
        ktq[:, o:o + nv] = ktb[:, :nv]
        vvq[o:o + nv] = vvb[:nv]
        ktq[:, o + SLOTK[s] - 512:o + SLOTK[s]] = ktb[:, nv:nv + 512]
        vvq[o + SLOTK[s] - 512:o + SLOTK[s]] = vvb[nv:nv + 512]
        t0 = o // 128
        kvm[:, t0 + nv // 128: t0 + (SLOTK[s] - 512) // 128] = -1e30
    return dict(qtq=qtq, ktq=np.ascontiguousarray(ktq), vvq=vvq, kvm=kvm), qbs




def build_mixout(nc, wname, wdtype, aname="at"):
    p = Prog(nc)
    x = nc.dram_tensor("x", [NTOK, 2048], F32, kind="ExternalInput").ap()
    at = nc.dram_tensor(aname, [2048, NTOK], BF16, kind="ExternalInput").ap()
    w = nc.dram_tensor(wname, [2048, 2048], wdtype, kind="ExternalInput").ap()
    xo = nc.dram_tensor("xo", [NTOK, 2048], F32, kind="ExternalOutput").ap()

    wb = p.sbuf("wb", [128, 16 * 2048], BF16)
    ab = p.sbuf("ab", [128, 16 * 128], BF16)
    mtok = p.sbuf("mtok", [128, 2048], F32)
    xtok = p.sbuf("xtok", [128, 2048], F32)
    lnout = p.sbuf("lnout", [128, 2048], F32)
    sq = p.sbuf("sqscr", [128, 2048], BF16)
    nm = p.sbuf("nm", [128, 1], F32)
    rs = p.sbuf("rs", [128, 1], F32)
    ssq = p.sbuf("ssq", [128, 1], F32)
    banks = [p.psum(f"bk{i}", [128, 512], F32) for i in range(8)]

    apw = bass.AP(w.tensor, 0, [[2048, 128], [128 * 2048, 16], [1, 2048]])
    eng = "gpsimd" if wdtype == F32 else "sync"
    p.add(eng, lambda g: g.dma_start(wb[:].rearrange("p (a b) -> p a b", b=2048), apw),
          w=["wb"], dma=True, tag="ldw")
    for ts in range(NTOK // 128):
        t0 = ts * 128
        apa = bass.AP(at.tensor, t0, [[NTOK, 128], [128 * NTOK, 16], [1, 128]])
        p.add("sync", lambda s, apa=apa: s.dma_start(
            ab[:].rearrange("p (a b) -> p a b", b=128), apa),
            w=["ab"], dma=True, tag=f"lda{ts}")
        for ob in range(4):
            bk = banks[ob % 2]
            for ct in range(16):
                p.add("tensor", lambda t, ct=ct, ob=ob, bk=bk:
                      t.matmul(bk[:], ab[:, ct * 128:(ct + 1) * 128],
                               wb[:, ct * 2048 + ob * 512: ct * 2048 + (ob + 1) * 512],
                               start=(ct == 0), stop=(ct == 15)),
                      r=["ab", "wb"], w=[bk.name], tag=f"mm{ts}_{ob}_{ct}")
            p.add("scalar", lambda s, ob=ob, bk=bk:
                  s.copy(mtok[:, ob * 512:(ob + 1) * 512], bk[:]),
                  r=[bk.name], w=["mtok"], tag=f"ev{ts}_{ob}")
        ln_tokmajor(p, mtok[:], "mtok", lnout[:], "lnout", nm, rs, sq, ssq,
                    2048, 1e-5, f"ln{ts}_")
        p.add("sync", lambda s, t0=t0: s.dma_start(xtok[:], x[t0:t0 + 128, :]),
              w=["xtok"], dma=True, tag=f"ldx{ts}")
        p.add("vector", lambda v: v.tensor_add(lnout[:], lnout[:], xtok[:]),
              r=["lnout", "xtok"], w=["lnout"], tag=f"add{ts}")
        p.add("sync", lambda s, t0=t0: s.dma_start(xo[t0:t0 + 128, :], lnout[:]),
              r=["lnout"], w=["xoout"], dma=True, tag=f"st{ts}")
    finish(p, ["xoout"])
    p.emit_all()




def build_l1ffa(nc):
    p = Prog(nc)
    x3 = nc.dram_tensor("x3", [NTOK, 2048], F32, kind="ExternalInput").ap()
    mut = nc.dram_tensor("mut", [2048, 2048], F32, kind="ExternalInput").ap()
    gamma = nc.dram_tensor("gamma", [1, 2048], F32, kind="ExternalInput").ap()
    beta = nc.dram_tensor("beta", [1, 2048], F32, kind="ExternalInput").ap()
    scale = nc.dram_tensor("scale", [1, 1], F32, kind="ExternalInput").ap()
    sinp = nc.dram_tensor("sinp", [2048, 2048], F32, kind="ExternalOutput").ap()
    denp = nc.dram_tensor("denp", [128, 16], F32, kind="ExternalOutput").ap()
    combt = nc.dram_tensor("combt", [2048, 2048], BF16, kind="ExternalOutput").ap()

    ident, identf = add_identity(p, nc)
    mub = p.sbuf("mub", [128, 16 * 16 * 128], BF16)
    muf = p.sbuf("muf", [128, 2048], F32)
    msq = p.sbuf("msq", [128, 2048], BF16)
    mun = p.sbuf("mun", [128, 2048], BF16)
    mss = p.sbuf("mss", [128, 1], F32)
    gb = p.sbuf("gb", [128, 2048], F32)
    bb = p.sbuf("bb", [128, 2048], F32)
    grow = p.sbuf("grow", [1, 2048], F32)
    srow = p.sbuf("srow", [1, 1], F32)
    scol = p.sbuf("scol", [128, 1], F32)
    onesrow = p.sbuf("onesrow", [1, 128], F32)
    onescol = p.sbuf("onescol", [128, 1], F32)
    onescolb = p.sbuf("onescolb", [128, 1], BF16)
    xtok = p.sbuf("xtok", [128, 2048], F32)
    lnout = p.sbuf("lnout", [128, 2048], F32)
    xlntb = p.sbuf("xlntb", [128, 4 * 2048], BF16)
    xn = p.sbuf("xn", [128, 2048], BF16)
    xnTb = p.sbuf("xnTb", [128, 16 * 512], BF16)
    dspE = p.sbuf("dspE", [128, 16 * 512], BF16)
    dspT = p.sbuf("dspT", [128, 4 * 2048], BF16)
    denacc = p.sbuf("denacc", [128, 16], F32)
    dcol = p.sbuf("dcol", [128, 1], F32)
    sinev = p.sbuf("sinev", [128, 512], F32)
    zt = p.sbuf("zt", [128, 2048], F32)
    sq = p.sbuf("sqscr", [128, 2048], BF16)
    nm = p.sbuf("nm", [128, 1], F32)
    rs = p.sbuf("rs", [128, 1], F32)
    ssq = p.sbuf("ssq", [128, 1], F32)
    banks = [p.psum(f"bk{i}", [128, 512], F32) for i in range(6)]
    tbk = [p.psum(f"tbk{i}", [128, 1024], BF16) for i in range(2)]
    crow = p.sbuf("crow", [1, 512], F32)

    p.add("vector", lambda v: v.memset(onesrow[:], 1.0), w=["onesrow"])
    p.add("vector", lambda v: v.memset(onescol[:], 1.0), w=["onescol"])
    p.add("vector", lambda v: v.memset(onescolb[:], 1.0), w=["onescolb"])
    p.add("vector", lambda v: v.memset(denacc[:], 0.0), w=["denacc"])
    p.add("vector", lambda v: v.memset(zt[:], 0.0), w=["zt"])
    for dt in range(16):
        p.add("sync", lambda s, dt=dt: s.dma_start(sinp[dt * 128:(dt + 1) * 128, :], zt[:]),
              r=["zt"], w=["sinpz"], dma=True, tag=f"z{dt}")
    p.add("sync", lambda s: s.dma_start(grow[:], gamma), w=["grow"], dma=True)
    p.add("sync", lambda s: s.dma_start(srow[:], scale), w=["srow"], dma=True)
    for ob in range(4):
        p.add("tensor", lambda t, ob=ob: t.matmul(
            banks[0][:], onesrow[:], grow[0:1, ob * 512:(ob + 1) * 512],
            start=True, stop=True), r=["onesrow", "grow"], w=["bk0"], tag=f"bg{ob}")
        p.add("scalar", lambda s, ob=ob: s.copy(gb[:, ob * 512:(ob + 1) * 512], banks[0][:]),
              r=["bk0"], w=["gb"], tag=f"bge{ob}")
    p.add("sync", lambda s: s.dma_start(grow[:], beta), r=["gb"], w=["grow"], dma=True)
    for ob in range(4):
        p.add("tensor", lambda t, ob=ob: t.matmul(
            banks[0][:], onesrow[:], grow[0:1, ob * 512:(ob + 1) * 512],
            start=True, stop=True), r=["onesrow", "grow"], w=["bk0"], tag=f"bb{ob}")
        p.add("scalar", lambda s, ob=ob: s.copy(bb[:, ob * 512:(ob + 1) * 512], banks[0][:]),
              r=["bk0"], w=["bb"], tag=f"bbe{ob}")
    p.add("tensor", lambda t: t.matmul(banks[0][:, 0:1], onesrow[:], srow[:],
                                       start=True, stop=True),
          r=["onesrow", "srow"], w=["bk0"], tag="bs")
    p.add("scalar", lambda s: s.copy(scol[:], banks[0][:, 0:1]), r=["bk0"], w=["scol"])
    for st in range(16):
        p.add("sync", lambda s, st=st: s.dma_start(muf[:], mut[st * 128:(st + 1) * 128, :]),
              w=["muf"], dma=True, tag=f"lmu{st}")
        p.add("scalar", lambda s: s.activation(msq[:], muf[:], AF.Square, accum_out=mss[:]),
              r=["muf"], w=["msq", "mss"], tag=f"ms{st}")
        p.add("scalar", lambda s: s.activation(mss[:], mss[:], AF.Sqrt),
              r=["mss"], w=["mss"], tag=f"msr{st}")
        p.add("vector", lambda v: v.tensor_scalar_max(mss[:], mss[:], 1e-12),
              r=["mss"], w=["mss"], tag=f"msm{st}")
        p.add("vector", lambda v: v.reciprocal(mss[:], mss[:]), r=["mss"], w=["mss"],
              tag=f"msc{st}")
        p.add("vector", lambda v: v.tensor_scalar(mun[:], muf[:], mss[:], None, ALU.mult),
              r=["muf", "mss"], w=["mun"], tag=f"mn{st}")
        for dt in range(16):
            bk = tbk[dt % 2]
            p.add("tensor", lambda t, dt=dt, bk=bk:
                  t.transpose(bk[:, 0:128], mun[:, dt * 128:(dt + 1) * 128], ident[:]),
                  r=["mun", "ident"], w=[bk.name], tag=f"mt{st}_{dt}")
            p.add("scalar", lambda s, dt=dt, st=st, bk=bk:
                  s.copy(mub[:, (dt * 16 + st) * 128:(dt * 16 + st) * 128 + 128],
                         bk[:, 0:128]),
                  r=[bk.name], w=["mub"], tag=f"mte{st}_{dt}")

    for tb in range(NTOK // 512):
        for sub in range(4):
            t0 = tb * 512 + sub * 128
            p.add("sync", lambda s, t0=t0: s.dma_start(xtok[:], x3[t0:t0 + 128, :]),
                  w=["xtok"], dma=True, tag=f"lx{tb}_{sub}")
            ln_tokmajor(p, xtok[:], "xtok", lnout[:], "lnout", nm, rs, sq, ssq,
                        2048, 1e-5, f"ln{tb}_{sub}")
            p.add("vector", lambda v: v.tensor_mul(lnout[:], lnout[:], gb[:]),
                  r=["lnout", "gb"], w=["lnout"], tag=f"g{tb}_{sub}")
            p.add("vector", lambda v: v.tensor_add(lnout[:], lnout[:], bb[:]),
                r=["lnout", "bb"], w=["lnout"], tag=f"b{tb}_{sub}")
            p.add("vector", lambda v, sub=sub: v.tensor_copy(
                xlntb[:, sub * 2048:(sub + 1) * 2048], lnout[:]),
                r=["lnout"], w=[f"xlntb{sub}"], tag=f"cb{tb}_{sub}")
            p.add("scalar", lambda s: s.activation(sq[:], lnout[:], AF.Square,
                                                   accum_out=ssq[:]),
                  r=["lnout"], w=["k_sqscr", "k_ssq"], tag=f"l2{tb}_{sub}")
            p.add("scalar", lambda s: s.activation(rs[:], ssq[:], AF.Sqrt),
                  r=["k_ssq"], w=["k_rstd"], tag=f"l2s{tb}_{sub}")
            p.add("vector", lambda v: v.tensor_scalar_max(rs[:], rs[:], 1e-12),
                  r=["k_rstd"], w=["k_rstd"], tag=f"l2m{tb}_{sub}")
            p.add("vector", lambda v: v.reciprocal(rs[:], rs[:]),
                  r=["k_rstd"], w=["k_rstd"], tag=f"l2r{tb}_{sub}")
            p.add("vector", lambda v: v.tensor_mul(rs[:], rs[:], scol[:]),
                  r=["k_rstd", "scol"], w=["k_rstd"], tag=f"l2sc{tb}_{sub}")
            p.add("vector", lambda v: v.tensor_scalar(xn[:], lnout[:], rs[:], None, ALU.mult),
                  r=["lnout", "k_rstd"], w=["xn"], tag=f"l2a{tb}_{sub}")
            for dt in range(16):
                bk = tbk[dt % 2]
                p.add("tensor", lambda t, dt=dt, bk=bk:
                      t.transpose(bk[:, 0:128], xn[:, dt * 128:(dt + 1) * 128], ident[:]),
                      r=["xn", "ident"], w=[bk.name], tag=f"xt{tb}_{sub}_{dt}")
                p.add("scalar", lambda s, dt=dt, sub=sub, bk=bk:
                      s.copy(xnTb[:, dt * 512 + sub * 128:dt * 512 + sub * 128 + 128],
                             bk[:, 0:128]),
                      r=[bk.name], w=[f"xnTb{dt}"], tag=f"xte{tb}_{sub}_{dt}")
        for st in range(16):
            bk = banks[2 + st % 2]
            for dt in range(16):
                p.add("tensor", lambda t, st=st, dt=dt, bk=bk:
                      t.matmul(bk[:], mub[:, (dt * 16 + st) * 128:(dt * 16 + st) * 128 + 128],
                               xnTb[:, dt * 512:(dt + 1) * 512],
                               start=(dt == 0), stop=(dt == 15)),
                      r=["mub", f"xnTb{dt}"], w=[bk.name], tag=f"lg{tb}_{st}_{dt}")
            p.add("scalar", lambda s, st=st, bk=bk:
                  s.activation(dspE[:, st * 512:(st + 1) * 512], bk[:], AF.Exp,
                               accum_out=dcol[:]),
                  r=[bk.name], w=[f"dspE{st}", "dcol"], tag=f"de{tb}_{st}")
            p.add("vector", lambda v, st=st: v.tensor_add(
                denacc[:, st:st + 1], denacc[:, st:st + 1], dcol[:]),
                r=["denacc", "dcol"], w=["denacc"], tag=f"da{tb}_{st}")
        for st in range(16):
            for sub in range(4):
                bk = tbk[(st * 4 + sub) % 2]
                p.add("tensor", lambda t, st=st, sub=sub, bk=bk:
                      t.transpose(bk[:, 0:128],
                                  dspE[:, st * 512 + sub * 128:st * 512 + sub * 128 + 128],
                                  ident[:]),
                      r=[f"dspE{st}", "ident"], w=[bk.name], tag=f"dt{tb}_{st}_{sub}")
                p.add("scalar", lambda s, st=st, sub=sub, bk=bk:
                      s.copy(dspT[:, sub * 2048 + st * 128:sub * 2048 + st * 128 + 128],
                             bk[:, 0:128]),
                      r=[bk.name], w=[f"dspT{sub}"], tag=f"dte{tb}_{st}_{sub}")
        for dt in range(16):
            for sb in range(4):
                bk = banks[sb % 2]
                for sub in range(4):
                    p.add("tensor", lambda t, dt=dt, sb=sb, sub=sub, bk=bk:
                          t.matmul(bk[:], xlntb[:, sub * 2048 + dt * 128:
                                               sub * 2048 + dt * 128 + 128],
                                   dspT[:, sub * 2048 + sb * 512:sub * 2048 + (sb + 1) * 512],
                                   start=(sub == 0), stop=(sub == 3)),
                          r=[f"xlntb{sub}", f"dspT{sub}"], w=[bk.name],
                          tag=f"si{tb}_{dt}_{sb}_{sub}")
                p.add("scalar", lambda s, bk=bk: s.copy(sinev[:], bk[:]),
                      r=[bk.name], w=["sinev"], tag=f"se{tb}_{dt}_{sb}")
                p.add("gpsimd", lambda g, dt=dt, sb=sb: g.dma_start(
                    sinp[dt * 128:(dt + 1) * 128, sb * 512:(sb + 1) * 512], sinev[:],
                    accum_op=ALU.add),
                    r=["sinev", "sinpz"], w=["sinpacc"], dma=True, tag=f"sa{tb}_{dt}_{sb}")
        for st in range(16):
            p.add("tensor", lambda t, st=st: t.matmul(
                banks[2][0:1, :], onescolb[:], dspE[:, st * 512:(st + 1) * 512],
                start=(st == 0), stop=(st == 15)),
                r=["onescolb", f"dspE{st}"], w=["bk2"], tag=f"cs{tb}_{st}")
        p.add("vector", lambda v: v.reciprocal(crow[:], banks[2][0:1, :]),
              r=["bk2"], w=["crow"], tag=f"cr{tb}")
        p.add("tensor", lambda t: t.matmul(banks[3][:], onesrow[:], crow[:],
                                           start=True, stop=True),
              r=["onesrow", "crow"], w=["bk3"], tag=f"cb{tb}")
        for st in range(16):
            p.add("vector", lambda v, st=st: v.tensor_mul(
                dspE[:, st * 512:(st + 1) * 512], dspE[:, st * 512:(st + 1) * 512],
                banks[3][:]),
                r=[f"dspE{st}", "bk3"], w=[f"dspE{st}"], tag=f"cn{tb}_{st}")
            p.add("sync", lambda s, st=st, tb=tb: s.dma_start(
                combt[st * 128:(st + 1) * 128, tb * 512:(tb + 1) * 512],
                dspE[:, st * 512:(st + 1) * 512]),
                r=[f"dspE{st}"], w=["combtout"], dma=True, tag=f"co{tb}_{st}")
    p.add("sync", lambda s: s.dma_start(denp, denacc[:]), r=["denacc"], w=["denpout"],
          dma=True, tag="sden")
    finish(p, ["combtout", "sinpacc", "denpout"])
    p.emit_all()




def build_l1exp(nc):
    p = Prog(nc)
    pin = [nc.dram_tensor(n, [2048, 512], F32, kind="ExternalInput").ap()
           for n in ["pa", "pb", "pc", "pd"]]
    den4 = nc.dram_tensor("den4", [4, 512], F32, kind="ExternalInput").ap()
    w1e = nc.dram_tensor("w1e", [2048, 2048], F32, kind="ExternalInput").ap()
    w2e = nc.dram_tensor("w2e", [2048, 2048], F32, kind="ExternalInput").ap()
    b1c = nc.dram_tensor("b1c", [128, 16], F32, kind="ExternalInput").ap()
    b2c = nc.dram_tensor("b2c", [128, 64], F32, kind="ExternalInput").ap()
    so = nc.dram_tensor("so", [2048, 512], BF16, kind="ExternalOutput").ap()

    pt = [p.sbuf(f"pt{i}", [128, 512], F32) for i in range(4)]
    sden = p.sbuf("sden", [4, 512], F32)
    ones4 = p.sbuf("ones4", [4, 1], F32)
    onesrow = p.sbuf("onesrow", [1, 128], F32)
    rrow = p.sbuf("rrow", [1, 512], F32)
    bden = p.sbuf("bden", [128, 512], F32)
    sinn = p.sbuf("sinn", [128, 16 * 512], BF16)
    w1b = p.sbuf("w1b", [128, 16 * 2048], BF16)
    w2b = p.sbuf("w2b", [128, 4 * 2048], BF16)
    sb1 = p.sbuf("sb1", [128, 16], F32)
    sb2 = p.sbuf("sb2", [128, 64], F32)
    hb = p.sbuf("hb", [128, 4 * 128], BF16)
    sout = p.sbuf("sout", [128, 128], BF16)
    banks = [p.psum(f"bk{i}", [128, 512], F32) for i in range(4)]

    p.add("vector", lambda v: v.memset(ones4[:], 1.0), w=["ones4"])
    p.add("vector", lambda v: v.memset(onesrow[:], 1.0), w=["onesrow"])
    p.add("sync", lambda s: s.dma_start(sden[:], den4), w=["sden"], dma=True)
    p.add("sync", lambda s: s.dma_start(sb1[:], b1c), w=["sb1"], dma=True)
    p.add("sync", lambda s: s.dma_start(sb2[:], b2c), w=["sb2"], dma=True)
    apw1 = bass.AP(w1e.tensor, 0, [[2048, 128], [128 * 2048, 16], [1, 2048]])
    p.add("gpsimd", lambda g: g.dma_start(w1b[:].rearrange("p (a b) -> p a b", b=2048), apw1),
          w=["w1b"], dma=True, tag="lw1")
    p.add("tensor", lambda t: t.matmul(banks[0][0:1, :], ones4[:], sden[:],
                                       start=True, stop=True),
          r=["ones4", "sden"], w=["bk0"], tag="dsum")
    p.add("vector", lambda v: v.reciprocal(rrow[:], banks[0][0:1, :]),
          r=["bk0"], w=["rrow"], tag="drec")
    p.add("tensor", lambda t: t.matmul(banks[1][:], onesrow[:], rrow[:],
                                       start=True, stop=True),
          r=["onesrow", "rrow"], w=["bk1"], tag="dbc")
    p.add("scalar", lambda s: s.copy(bden[:], banks[1][:]), r=["bk1"], w=["bden"], tag="dbe")
    for dt in range(16):
        for i in range(4):
            p.add("sync", lambda s, dt=dt, i=i: s.dma_start(
                pt[i][:], pin[i][dt * 128:(dt + 1) * 128, :]),
                w=[f"pt{i}"], dma=True, tag=f"lp{dt}_{i}")
        p.add("vector", lambda v: v.tensor_add(pt[0][:], pt[0][:], pt[1][:]),
              r=["pt0", "pt1"], w=["pt0"], tag=f"a1{dt}")
        p.add("vector", lambda v: v.tensor_add(pt[2][:], pt[2][:], pt[3][:]),
              r=["pt2", "pt3"], w=["pt2"], tag=f"a2{dt}")
        p.add("vector", lambda v: v.tensor_add(pt[0][:], pt[0][:], pt[2][:]),
              r=["pt0", "pt2"], w=["pt0"], tag=f"a3{dt}")
        p.add("vector", lambda v, dt=dt: v.tensor_mul(
            sinn[:, dt * 512:(dt + 1) * 512], pt[0][:], bden[:]),
            r=["pt0", "bden"], w=[f"sinn{dt}"], tag=f"nn{dt}")
    for e in range(4):
        apw2 = bass.AP(w2e.tensor, e * 512 * 2048, [[2048, 128], [128 * 2048, 4], [1, 2048]])
        p.add("gpsimd", lambda g, apw2=apw2: g.dma_start(
            w2b[:].rearrange("p (a b) -> p a b", b=2048), apw2),
            w=["w2b"], dma=True, tag=f"lw2{e}")
        for ht in range(4):
            bk = banks[2 + ht % 2]
            for dt in range(16):
                p.add("tensor", lambda t, e=e, ht=ht, dt=dt, bk=bk:
                      t.matmul(bk[:, 0:128],
                               w1b[:, dt * 2048 + e * 512 + ht * 128:
                                   dt * 2048 + e * 512 + (ht + 1) * 128],
                               sinn[:, dt * 512 + e * 128:dt * 512 + (e + 1) * 128],
                               start=(dt == 0), stop=(dt == 15)),
                      r=["w1b", f"sinn{dt}"], w=[bk.name], tag=f"h{e}_{ht}_{dt}")
            p.add("scalar", lambda s, e=e, ht=ht, bk=bk:
                  s.activation(hb[:, ht * 128:(ht + 1) * 128], bk[:, 0:128], AF.Gelu,
                               bias=sb1[:, e * 4 + ht:e * 4 + ht + 1]),
                  r=[bk.name, "sb1"], w=[f"hb{ht}"], tag=f"he{e}_{ht}")
        for dto in range(16):
            bk = banks[2 + dto % 2]
            for ht in range(4):
                p.add("tensor", lambda t, e=e, ht=ht, dto=dto, bk=bk:
                      t.matmul(bk[:, 0:128],
                               w2b[:, ht * 2048 + dto * 128:ht * 2048 + (dto + 1) * 128],
                               hb[:, ht * 128:(ht + 1) * 128],
                               start=(ht == 0), stop=(ht == 3)),
                      r=["w2b", f"hb{ht}"], w=[bk.name], tag=f"o{e}_{dto}_{ht}")
            p.add("scalar", lambda s, e=e, dto=dto, bk=bk:
                  s.activation(sout[:], bk[:, 0:128], AF.Identity,
                               bias=sb2[:, e * 16 + dto:e * 16 + dto + 1]),
                  r=[bk.name, "sb2"], w=["sout"], tag=f"oe{e}_{dto}")
            p.add("sync", lambda s, e=e, dto=dto: s.dma_start(
                so[dto * 128:(dto + 1) * 128, e * 128:(e + 1) * 128], sout[:]),
                r=["sout"], w=["soout"], dma=True, tag=f"os{e}_{dto}")
    finish(p, ["soout"])
    p.emit_all()



MIXNAMES = ["xr", "xw", "xk", "xv", "xa", "xg"]


def build_l0mix(nc):
    p = Prog(nc)
    x = nc.dram_tensor("x", [NTOK + 1, 2048], F32, kind="ExternalInput").ap()
    mixc = nc.dram_tensor("mixc", [128, 96], F32, kind="ExternalInput").ap()
    outs = {n: nc.dram_tensor(n, [2048, NTOK], BF16, kind="ExternalOutput").ap()
            for n in MIXNAMES}

    ident, identf = add_identity(p, nc)
    xtok = p.sbuf("xtok", [128, 2048], F32)
    xln = p.sbuf("xln", [128, 2048], F32)
    sq = p.sbuf("sqscr", [128, 2048], BF16)
    nm = p.sbuf("nm", [128, 1], F32)
    rs = p.sbuf("rs", [128, 1], F32)
    ssq = p.sbuf("ssq", [128, 1], F32)
    smix = p.sbuf("smix", [128, 96], F32)
    halo = p.sbuf("halo", [1, 2048], F32)
    halob = p.sbuf("halob", [1, 2048], F32)
    onesone = p.sbuf("onesone", [1, 1], F32)
    xlnT = p.sbuf("xlnT", [128, 16 * 513], F32)
    xx = p.sbuf("xx", [128, 512], F32)
    xm = p.sbuf("xm", [128, 512], BF16)
    banks = [p.psum(f"bk{i}", [128, 512], F32) for i in range(4)]

    p.add("sync", lambda s: s.dma_start(smix[:], mixc), w=["smix"], dma=True)
    p.add("vector", lambda v: v.memset(onesone[:], 1.0), w=["onesone"])
    p.add("sync", lambda s: s.dma_start(halo[:], x[0:1, :]), w=["halo"], dma=True)
    ln_tokmajor(p, halo[0:1, :], "halo", halob[0:1, :], "halob", nm, rs, sq, ssq,
                2048, 1e-5, "lnh", nt=1)
    for dt in range(16):
        p.add("tensor", lambda t, dt=dt: t.matmul(
            banks[3][0:128, 0:1], halob[0:1, dt * 128:(dt + 1) * 128], onesone[:],
            start=True, stop=True),
            r=["halob", "onesone"], w=["bk3"], tag=f"hc{dt}")
        p.add("scalar", lambda s, dt=dt: s.copy(xlnT[:, dt * 513:dt * 513 + 1],
                                                banks[3][0:128, 0:1]),
              r=["bk3"], w=["xlnThalo"], tag=f"hce{dt}")
    for tb in range(NTOK // 512):
        if tb > 0:
            for dt in range(16):
                p.add("vector", lambda v, dt=dt: v.tensor_copy(
                    xlnT[:, dt * 513:dt * 513 + 1], xlnT[:, dt * 513 + 512:dt * 513 + 513]),
                    r=[f"xlnT{dt}"], w=["xlnThalo"], tag=f"hcp{tb}_{dt}")
        for sub in range(4):
            t0 = tb * 512 + sub * 128
            p.add("sync", lambda s, t0=t0: s.dma_start(xtok[:], x[t0 + 1:t0 + 129, :]),
                  w=["xtok"], dma=True, tag=f"lx{tb}_{sub}")
            ln_tokmajor(p, xtok[:], "xtok", xln[:], "xln", nm, rs, sq, ssq,
                        2048, 1e-5, f"ln{tb}_{sub}")
            for dt in range(16):
                bk = banks[dt % 2]
                p.add("tensor", lambda t, dt=dt, bk=bk:
                      t.transpose(bk[:, 0:128], xln[:, dt * 128:(dt + 1) * 128],
                                  identf[:]),
                      r=["xln", "identf"], w=[bk.name], tag=f"tp{tb}_{sub}_{dt}")
                p.add("scalar", lambda s, dt=dt, sub=sub, bk=bk:
                      s.copy(xlnT[:, dt * 513 + 1 + sub * 128:dt * 513 + 1 + sub * 128 + 128],
                             bk[:, 0:128]),
                      r=[bk.name, "xlnThalo"], w=[f"xlnT{dt}"], tag=f"ev{tb}_{sub}_{dt}")
        for dt in range(16):
            p.add("vector", lambda v, dt=dt: v.tensor_sub(
                xx[:], xlnT[:, dt * 513:dt * 513 + 512], xlnT[:, dt * 513 + 1:dt * 513 + 513]),
                r=[f"xlnT{dt}", "xlnThalo"], w=["xx"], tag=f"xx{tb}_{dt}")
            for j, n in enumerate(MIXNAMES):
                p.add("vector", lambda v, dt=dt, j=j: v.scalar_tensor_tensor(
                    xm[:], xx[:], smix[:, 16 * j + dt:16 * j + dt + 1],
                    xlnT[:, dt * 513 + 1:dt * 513 + 513], ALU.mult, ALU.add),
                    r=["xx", "smix", f"xlnT{dt}"], w=["xm"], tag=f"m{tb}_{dt}_{j}")
                p.add("sync", lambda s, dt=dt, tb=tb, n=n: s.dma_start(
                    outs[n][dt * 128:(dt + 1) * 128, tb * 512:(tb + 1) * 512], xm[:]),
                    r=["xm"], w=["mixout"], dma=True, tag=f"sm{tb}_{dt}_{j}")
    finish(p, ["mixout"])
    p.emit_all()




def build_l0rec(nc):
    p = Prog(nc)
    mixes = {n: nc.dram_tensor(n, [2048, 8192], BF16, kind="ExternalInput").ap()
             for n in ["mr", "mw", "mk", "mv", "ma", "mg"]}
    wr = nc.dram_tensor("wr", [2048, 512], F32, kind="ExternalInput").ap()
    wk = nc.dram_tensor("wk", [2048, 512], F32, kind="ExternalInput").ap()
    wv = nc.dram_tensor("wv", [2048, 512], F32, kind="ExternalInput").ap()
    w1 = nc.dram_tensor("w1", [2048, 64], F32, kind="ExternalInput").ap()
    w2 = nc.dram_tensor("w2", [64, 512], F32, kind="ExternalInput").ap()
    a1 = nc.dram_tensor("a1", [2048, 64], F32, kind="ExternalInput").ap()
    a2 = nc.dram_tensor("a2", [64, 512], F32, kind="ExternalInput").ap()
    g1 = nc.dram_tensor("g1", [2048, 128], F32, kind="ExternalInput").ap()
    g2 = nc.dram_tensor("g2", [128, 512], F32, kind="ExternalInput").ap()
    colsi = nc.dram_tensor("cols", [128, 32], F32, kind="ExternalInput").ap()
    og = nc.dram_tensor("og", [512, 8192], BF16, kind="ExternalOutput").ap()

    ident, identf = add_identity(p, nc)
    CO = p.sbuf("CO", [128, 32], F32)
    wslab = p.sbuf("wslab", [128, 16 * 512], BF16)
    w1b = p.sbuf("w1b", [128, 16 * 64], BF16)
    w2b = p.sbuf("w2b", [64, 512], BF16)
    a1b = p.sbuf("a1b", [128, 16 * 64], BF16)
    a2b = p.sbuf("a2b", [64, 512], BF16)
    g1b = p.sbuf("g1b", [128, 16 * 128], BF16)
    g2b = p.sbuf("g2b", [128, 512], BF16)
    mixb = p.sbuf("mixb", [128, 16 * 512], BF16)
    rb = p.sbuf("rb", [128, 2048], BF16)
    kraw = p.sbuf("kraw", [128, 2048], BF16)
    kpb = p.sbuf("kpb", [128, 2048], BF16)
    vb = p.sbuf("vb", [128, 2048], BF16)
    a4b = p.sbuf("a4b", [128, 2048], BF16)
    b4b = p.sbuf("b4b", [128, 2048], BF16)
    aab = p.sbuf("aab", [128, 2048], BF16)
    ggb = p.sbuf("ggb", [128, 2048], BF16)
    ewf = p.sbuf("ewf", [128, 2048], F32)
    wdecf = p.sbuf("wdecf", [128, 2048], F32)
    cF = p.sbuf("cF", [128, 2048], F32)
    cinvF = p.sbuf("cinvF", [128, 2048], F32)
    cexclF = p.sbuf("cexclF", [128, 2048], F32)
    KhatF = p.sbuf("KhatF", [128, 2048], BF16)
    BhatF = p.sbuf("BhatF", [128, 2048], BF16)
    AqF = p.sbuf("AqF", [128, 2048], BF16)
    RqF = p.sbuf("RqF", [128, 2048], BF16)
    OGf = p.sbuf("OGf", [128, 2048], F32)
    z1b = p.sbuf("z1b", [64, 512], BF16)
    t1 = p.sbuf("t1", [128, 2048], F32)
    t2 = p.sbuf("t2", [128, 2048], F32)
    rowa = p.sbuf("rowa", [1, 512], F32)
    rowb = p.sbuf("rowb", [1, 512], F32)
    rowc = p.sbuf("rowc", [1, 512], F32)
    onescol = p.sbuf("onescol", [128, 1], F32)
    onescb = p.sbuf("onescb", [128, 1], BF16)
    onesrow = p.sbuf("onesrow", [1, 64], F32)
    Kc = p.sbuf("Kc", [128, 128], BF16)
    Bc = p.sbuf("Bc", [128, 128], BF16)
    diagp = p.sbuf("diagp", [128, 128], BF16)
    VTs = p.sbuf("VTs", [128, 128], BF16)
    BcT = p.sbuf("BcT", [128, 128], BF16)
    KcT = p.sbuf("KcT", [128, 128], BF16)
    Usb = p.sbuf("Usb", [128, 128], BF16)
    UsbT = p.sbuf("UsbT", [128, 128], BF16)
    Ukb = p.sbuf("Ukb", [128, 128], BF16)
    Mbr = p.sbuf("Mbr", [128, 128], BF16)
    Mkr = p.sbuf("Mkr", [128, 128], BF16)
    Ast = p.sbuf("Ast", [128, 6 * 128], BF16)
    Bst = p.sbuf("Bst", [128, 2 * 128], BF16)
    Wb = p.sbuf("Wb", [128, 64], BF16)
    SAb = p.sbuf("SAb", [128, 64], BF16)
    SLf = p.sbuf("SLf", [128, 64], F32)
    S0Tp = [p.sbuf(f"S0T{i}", [128, 64], F32) for i in range(4)]
    S0Tb = p.sbuf("S0Tb", [128, 64], BF16)
    maskSU = p.sbuf("maskSU", [128, 128], F32)
    maskSL = p.sbuf("maskSL", [128, 128], F32)
    maskIU = p.sbuf("maskIU", [128, 128], F32)
    ident2 = p.sbuf("ident2", [128, 64], BF16)
    iotat = p.sbuf("iotat", [128, 128], F32)
    pb = [p.psum(f"pb{i}", [128, 512], F32) for i in range(2)]
    wbk = p.psum("wbk", [128, 512], F32)
    obk = p.psum("obk", [128, 512], F32)
    sbk = p.psum("sbk", [128, 512], F32)
    xbk = p.psum("xbk", [128, 512], F32)
    tb0 = p.psum("tb0", [128, 1024], BF16)
    tb1 = p.psum("tb1", [128, 1024], BF16)

    p.add("sync", lambda s: s.dma_start(CO[:], colsi), w=["CO"], dma=True)
    p.add("vector", lambda v: v.tensor_scalar(CO[:, 28:32], CO[:, 12:16],
                                              -1.0, 1.0, ALU.mult, ALU.add),
          r=["CO"], w=["CO"], tag="coka")
    p.add("vector", lambda v: v.memset(onescol[:], 1.0), w=["onescol"])
    p.add("vector", lambda v: v.memset(onescb[:], 1.0), w=["onescb"])
    p.add("vector", lambda v: v.memset(onesrow[:], 1.0), w=["onesrow"])
    for i in range(4):
        p.add("vector", lambda v, i=i: v.memset(S0Tp[i][:], 0.0), w=[f"S0T{i}"])
    p.add("gpsimd", lambda g: g.iota(iotat[:], [[-1, 128]], base=0, channel_multiplier=1,
                                     allow_small_or_imprecise_dtypes=True), w=["iotat"])
    p.add("vector", lambda v: v.tensor_single_scalar(maskSU[:], iotat[:], 0.0, ALU.is_lt),
          r=["iotat"], w=["maskSU"])
    p.add("vector", lambda v: v.tensor_single_scalar(maskSL[:], iotat[:], 0.0, ALU.is_gt),
          r=["iotat"], w=["maskSL"])
    p.add("vector", lambda v: v.tensor_single_scalar(maskIU[:], iotat[:], 0.0, ALU.is_le),
          r=["iotat"], w=["maskIU"])
    p.add("vector", lambda v: v.tensor_single_scalar(t1[:, 0:64], iotat[:, 0:64], 0.0,
                                                     ALU.is_equal),
          r=["iotat"], w=["t1"])
    p.add("vector", lambda v: v.tensor_single_scalar(t2[:, 0:64], iotat[:, 0:64], 64.0,
                                                     ALU.is_equal),
          r=["iotat"], w=["t2"])
    p.add("vector", lambda v: v.tensor_add(ident2[:], t1[:, 0:64], t2[:, 0:64]),
          r=["t1", "t2"], w=["ident2"])

    def ldw(dst, dstk, src_t, nrow, ncol, tag):
        ap = bass.AP(src_t.tensor, 0, [[ncol, 128], [128 * ncol, nrow // 128], [1, ncol]]) \
            if nrow > 128 else bass.AP(src_t.tensor, 0, [[ncol, nrow], [1, ncol]])
        d = dst[0:min(nrow, 128), 0:(nrow // 128) * ncol if nrow > 128 else ncol]
        d3 = d.rearrange("p (a b) -> p a b", b=ncol) if nrow > 128 else d
        p.add("gpsimd", lambda g, d3=d3, ap=ap: g.dma_start(d3, ap),
              w=[dstk], dma=True, tag=tag)

    ldw(w1b, "w1b", w1, 2048, 64, "lw1")
    ldw(w2b, "w2b", w2, 64, 512, "lw2")
    ldw(a1b, "a1b", a1, 2048, 64, "la1")
    ldw(a2b, "a2b", a2, 64, 512, "la2")
    ldw(g1b, "g1b", g1, 2048, 128, "lg1")
    ldw(g2b, "g2b", g2, 128, 512, "lg2")

    def loadmix(n, tb, tag):
        ap = bass.AP(mixes[n].tensor, tb * 512, [[8192, 128], [128 * 8192, 16], [1, 512]])
        p.add("sync", lambda s, ap=ap: s.dma_start(
            mixb[:].rearrange("p (a b) -> p a b", b=512), ap),
            w=["mixb"], dma=True, tag=tag)

    def proj512(dst, dstk, wtile, wtk, tb, tag, dt_n=16):
        for f4 in range(4):
            bk = pb[f4 % 2]
            for dt in range(dt_n):
                p.add("tensor", lambda t, f4=f4, dt=dt, bk=bk, wtile=wtile:
                      t.matmul(bk[:], wtile[:, dt * 512 + f4 * 128:dt * 512 + f4 * 128 + 128],
                               mixb[:, dt * 512:(dt + 1) * 512],
                               start=(dt == 0), stop=(dt == dt_n - 1)),
                      r=[wtk, "mixb"], w=[bk.name], tag=f"{tag}m{f4}_{dt}")
            p.add("scalar", lambda s, f4=f4, bk=bk, dst=dst: s.copy(
                dst[:, f4 * 512:(f4 + 1) * 512], bk[:]),
                r=[bk.name], w=[dstk], tag=f"{tag}e{f4}")

    E = dict(cF=cF, cinvF=cinvF, cexclF=cexclF, KhatF=KhatF, BhatF=BhatF, AqF=AqF,
             RqF=RqF, vb=vb, rb=rb, kpb=kpb, b4b=b4b, ggb=ggb, OGf=OGf, Kc=Kc, Bc=Bc,
             diagp=diagp, VTs=VTs, BcT=BcT, KcT=KcT, Usb=Usb, UsbT=UsbT, Ukb=Ukb,
             Mbr=Mbr, Mkr=Mkr, Ast=Ast, Bst=Bst, Wb=Wb, SAb=SAb, SLf=SLf, S0Tp=S0Tp,
             S0Tb=S0Tb, maskSU=maskSU, maskSL=maskSL, maskIU=maskIU, ident2=ident2,
             identf=identf, ident=ident, pb=pb, wbk=wbk, obk=obk, sbk=sbk, xbk=xbk,
             tb0=tb0, tb1=tb1, t1=t1, t2=t2, rowa=rowa, rowb=rowb, rowc=rowc,
             onescol=onescol, onescb=onescb, onesrow=onesrow, CO=CO, og=og)

    ep1 = _eps_col(p, 1.0)
    epn05 = _eps_col(p, -0.5)
    NB = 8192 // 512
    for tb in range(NB):
        loadmix("mw", tb, f"lmw{tb}")
        for dt in range(16):
            p.add("tensor", lambda t, dt=dt: t.matmul(
                xbk[0:64, :], w1b[:, dt * 64:(dt + 1) * 64], mixb[:, dt * 512:(dt + 1) * 512],
                start=(dt == 0), stop=(dt == 15)),
                r=["w1b", "mixb"], w=["xbk"], tag=f"z1m{tb}_{dt}")
        p.add("scalar", lambda s: s.activation(z1b[:], xbk[0:64, :], AF.Tanh),
              r=["xbk"], w=["z1b"], tag=f"z1t{tb}")
        for f4 in range(4):
            bk = pb[f4 % 2]
            p.add("tensor", lambda t, f4=f4, bk=bk: t.matmul(
                bk[:], w2b[:, f4 * 128:(f4 + 1) * 128], z1b[:],
                start=True, stop=True),
                r=["w2b", "z1b"], w=[bk.name], tag=f"wr{tb}_{f4}")
            p.add("scalar", lambda s, f4=f4, bk=bk: s.activation(
                t1[:, f4 * 512:(f4 + 1) * 512], bk[:], AF.Identity,
                bias=CO[:, f4:f4 + 1]),
                r=[bk.name, "CO"], w=["t1"], tag=f"wb{tb}_{f4}")
        p.add("scalar", lambda s: s.activation(t2[:], t1[:], AF.Abs),
              r=["t1"], w=["t2"], tag=f"sp1{tb}")
        p.add("scalar", lambda s: s.activation(t2[:], t2[:], AF.Exp, scale=-1.0),
              r=["t2"], w=["t2"], tag=f"sp2{tb}")
        p.add("scalar", lambda s: s.activation(t2[:], t2[:], AF.Ln, bias=ep1[:]),
              r=["t2", "epscol1.0"], w=["t2"], tag=f"sp3{tb}")
        p.add("scalar", lambda s: s.activation(ewf[:], t1[:], AF.Relu, scale=-1.0),
              r=["t1"], w=["ewf"], tag=f"sp4{tb}")
        p.add("vector", lambda v: v.tensor_add(ewf[:], ewf[:], t2[:]),
              r=["ewf", "t2"], w=["ewf"], tag=f"sp5{tb}")
        p.add("scalar", lambda s: s.activation(ewf[:], ewf[:], AF.Exp, scale=-1.0,
                                               bias=epn05[:]),
              r=["ewf", "epscol-0.5"], w=["ewf"], tag=f"sp6{tb}")
        p.add("scalar", lambda s: s.activation(wdecf[:], ewf[:], AF.Exp, scale=-1.0),
              r=["ewf"], w=["wdecf"], tag=f"wd{tb}")
        loadmix("ma", tb, f"lma{tb}")
        for dt in range(16):
            p.add("tensor", lambda t, dt=dt: t.matmul(
                xbk[0:64, :], a1b[:, dt * 64:(dt + 1) * 64], mixb[:, dt * 512:(dt + 1) * 512],
                start=(dt == 0), stop=(dt == 15)),
                r=["a1b", "mixb"], w=["xbk"], tag=f"zam{tb}_{dt}")
        p.add("scalar", lambda s: s.copy(z1b[:], xbk[0:64, :]), r=["xbk"], w=["z1b"],
              tag=f"zae{tb}")
        for f4 in range(4):
            bk = pb[f4 % 2]
            p.add("tensor", lambda t, f4=f4, bk=bk: t.matmul(
                bk[:], a2b[:, f4 * 128:(f4 + 1) * 128], z1b[:], start=True, stop=True),
                r=["a2b", "z1b"], w=[bk.name], tag=f"ar{tb}_{f4}")
            p.add("scalar", lambda s, f4=f4, bk=bk: s.activation(
                aab[:, f4 * 512:(f4 + 1) * 512], bk[:], AF.Sigmoid,
                bias=CO[:, 4 + f4:5 + f4]),
                r=[bk.name, "CO"], w=["aab"], tag=f"ab{tb}_{f4}")
        loadmix("mg", tb, f"lmg{tb}")
        for dt in range(16):
            p.add("tensor", lambda t, dt=dt: t.matmul(
                xbk[:], g1b[:, dt * 128:(dt + 1) * 128], mixb[:, dt * 512:(dt + 1) * 512],
                start=(dt == 0), stop=(dt == 15)),
                r=["g1b", "mixb"], w=["xbk"], tag=f"zgm{tb}_{dt}")
        p.add("scalar", lambda s: s.activation(t1[:, 0:512].bitcast(BF16)[:, 0:512],
                                               xbk[:], AF.Sigmoid),
              r=["xbk"], w=["t1"], tag=f"zge{tb}")
        for f4 in range(4):
            bk = pb[f4 % 2]
            p.add("tensor", lambda t, f4=f4, bk=bk: t.matmul(
                bk[:], g2b[:, f4 * 128:(f4 + 1) * 128],
                t1[:, 0:512].bitcast(BF16)[:, 0:512], start=True, stop=True),
                r=["g2b", "t1"], w=[bk.name], tag=f"gg{tb}_{f4}")
            p.add("scalar", lambda s, f4=f4, bk=bk: s.copy(
                ggb[:, f4 * 512:(f4 + 1) * 512], bk[:]),
                r=[bk.name], w=["ggb"], tag=f"gge{tb}_{f4}")
        def ldslab(srct, tag):
            ap = bass.AP(srct.tensor, 0, [[512, 128], [128 * 512, 16], [1, 512]])
            p.add("gpsimd", lambda gg, ap=ap: gg.dma_start(
                wslab[:].rearrange("p (a b) -> p a b", b=512), ap),
                w=["wslab"], dma=True, tag=tag)
        loadmix("mr", tb, f"lmr{tb}")
        ldslab(wr, f"lwr{tb}")
        proj512(rb, "rb", wslab, "wslab", tb, f"pr{tb}_")
        loadmix("mk", tb, f"lmk{tb}")
        ldslab(wk, f"lwk{tb}")
        for f4 in range(4):
            bk = pb[f4 % 2]
            for dt in range(16):
                p.add("tensor", lambda t, f4=f4, dt=dt, bk=bk:
                      t.matmul(bk[:], wslab[:, dt * 512 + f4 * 128:dt * 512 + f4 * 128 + 128],
                               mixb[:, dt * 512:(dt + 1) * 512],
                               start=(dt == 0), stop=(dt == 15)),
                      r=["wslab", "mixb"], w=[bk.name], tag=f"pk{tb}m{f4}_{dt}")
            p.add("scalar", lambda s, f4=f4, bk=bk: s.copy(
                kraw[:, f4 * 512:(f4 + 1) * 512], bk[:]),
                r=[bk.name], w=["kraw"], tag=f"pk{tb}e{f4}")
        loadmix("mv", tb, f"lmv{tb}")
        ldslab(wv, f"lwv{tb}")
        proj512(vb, "vb", wslab, "wslab", tb, f"pv{tb}_")
        for f4 in range(4):
            p.add("vector", lambda v, f4=f4: v.tensor_scalar(
                t1[:, f4 * 512:(f4 + 1) * 512], kraw[:, f4 * 512:(f4 + 1) * 512],
                CO[:, 8 + f4:9 + f4], None, ALU.mult),
                r=["kraw", "CO"], w=["t1"], tag=f"kk1{tb}_{f4}")
        p.add("vector", lambda v: v.tensor_mul(t2[:], t1[:], t1[:]),
              r=["t1"], w=["t2"], tag=f"kk2{tb}")
        for f4 in range(4):
            for hh in (0, 64):
                p.add("tensor", lambda t, f4=f4, hh=hh: t.matmul(
                    xbk[0:1, :], onescol[hh:hh + 64, :],
                    t2[hh:hh + 64, f4 * 512:(f4 + 1) * 512], start=True, stop=True),
                    r=["onescol", "t2"], w=["xbk"], tag=f"kk3{tb}_{f4}_{hh}")
                p.add("vector", lambda v: v.tensor_copy(rowa[:], xbk[0:1, :]),
                      r=["xbk"], w=["rowa"], tag=f"kk4{tb}_{f4}_{hh}")
                p.add("scalar", lambda s: s.activation(rowa[:], rowa[:], AF.Sqrt),
                      r=["rowa"], w=["rowa"], tag=f"kk5{tb}_{f4}_{hh}")
                p.add("vector", lambda v: v.tensor_scalar_max(rowa[:], rowa[:], 1e-12),
                      r=["rowa"], w=["rowa"], tag=f"kk6{tb}_{f4}_{hh}")
                p.add("vector", lambda v: v.reciprocal(rowa[:], rowa[:]),
                      r=["rowa"], w=["rowa"], tag=f"kk7{tb}_{f4}_{hh}")
                p.add("tensor", lambda t, hh=hh: t.matmul(
                    xbk[hh:hh + 64, :], onesrow[:], rowa[:], start=True, stop=True),
                    r=["onesrow", "rowa"], w=["xbk"], tag=f"kk8{tb}_{f4}_{hh}")
                p.add("vector", lambda v, f4=f4, hh=hh: v.tensor_mul(
                    t2[hh:hh + 64, f4 * 512:(f4 + 1) * 512],
                    t1[hh:hh + 64, f4 * 512:(f4 + 1) * 512], xbk[hh:hh + 64, :]),
                    r=["t1", "xbk"], w=["t2"], tag=f"kk9{tb}_{f4}_{hh}")
        p.add("vector", lambda v: v.tensor_scalar_mul(a4b[:], t2[:], -1.0),
              r=["t2"], w=["a4b"], tag=f"a4{tb}")
        p.add("vector", lambda v: v.tensor_mul(b4b[:], t2[:], aab[:]),
              r=["t2", "aab"], w=["b4b"], tag=f"b4{tb}")
        for f4 in range(4):
            p.add("vector", lambda v, f4=f4: v.tensor_scalar(
                t1[:, f4 * 512:(f4 + 1) * 512], aab[:, f4 * 512:(f4 + 1) * 512],
                CO[:, 12 + f4:13 + f4], CO[:, 28 + f4:29 + f4], ALU.mult, ALU.add),
                r=["aab", "CO"], w=["t1"], tag=f"kp1{tb}_{f4}")
        p.add("vector", lambda v: v.tensor_mul(kpb[:], kraw[:], t1[:]),
              r=["kraw", "t1"], w=["kpb"], tag=f"kp3{tb}")
        for pt in range(4):
            for ch in range(4):
                sl = slice(pt * 512 + ch * 128, pt * 512 + (ch + 1) * 128)
                p.add("vector", lambda v, sl=sl: v.tensor_tensor_scan(
                    cF[:, sl], wdecf[:, sl], wdecf[:, sl], 1.0, ALU.mult, ALU.bypass),
                    r=["wdecf"], w=["cF"], tag=f"sc{tb}_{pt}_{ch}")
        p.add("vector", lambda v: v.reciprocal(cinvF[:], cF[:]),
              r=["cF"], w=["cinvF"], tag=f"ci{tb}")
        for pt4 in range(4):
            for ch4 in range(4):
                c04 = pt4 * 512 + ch4 * 128
                p.add("vector", lambda v, c04=c04: v.tensor_copy(
                    cexclF[:, c04 + 1:c04 + 128], cF[:, c04:c04 + 127]),
                    r=["cF"], w=["cexclF"], tag=f"ce{tb}_{pt4}_{ch4}")
                p.add("vector", lambda v, c04=c04: v.memset(cexclF[:, c04:c04 + 1], 1.0),
                    w=["cexclF"], tag=f"ce1{tb}_{pt4}_{ch4}")
        p.add("vector", lambda v: v.tensor_mul(KhatF[:], kpb[:], cinvF[:]),
              r=["kpb", "cinvF"], w=["KhatF"], tag=f"dk{tb}")
        p.add("vector", lambda v: v.tensor_mul(BhatF[:], b4b[:], cinvF[:]),
              r=["b4b", "cinvF"], w=["BhatF"], tag=f"db{tb}")
        p.add("vector", lambda v: v.tensor_mul(AqF[:], a4b[:], cexclF[:]),
              r=["a4b", "cexclF"], w=["AqF"], tag=f"da{tb}")
        p.add("vector", lambda v: v.tensor_mul(RqF[:], rb[:], cF[:]),
              r=["rb", "cF"], w=["RqF"], tag=f"dr{tb}")
        for pt in range(4):
            for ch in range(4):
                chunk_step(p, E, tb, pt, ch)
        post_block(p, E, tb)
    finish(p, ["ogout"])
    p.emit_all()


def chunk_step(p, E, tb, pt, ch):
    g = lambda n: E[n]
    c0 = pt * 512 + ch * 128
    sl = slice(c0, c0 + 128)
    cF, KhatF, BhatF, AqF, RqF, vb = g("cF"), g("KhatF"), g("BhatF"), g("AqF"), g("RqF"), g("vb")
    Kc, Bc, diagp, VTs, BcT, KcT = g("Kc"), g("Bc"), g("diagp"), g("VTs"), g("BcT"), g("KcT")
    Usb, UsbT, Ukb, Mbr, Mkr = g("Usb"), g("UsbT"), g("Ukb"), g("Mbr"), g("Mkr")
    Ast, Bst, Wb, SAb, SLf = g("Ast"), g("Bst"), g("Wb"), g("SAb"), g("SLf")
    S0Tp, S0Tb = g("S0Tp"), g("S0Tb")
    pb, wbk, obk, sbk, xbk, tb0, tb1 = g("pb"), g("wbk"), g("obk"), g("sbk"), g("xbk"), g("tb0"), g("tb1")
    identf, ident, ident2 = g("identf"), g("ident"), g("ident2")
    maskSU, maskSL, maskIU = g("maskSU"), g("maskSL"), g("maskIU")
    OGf = g("OGf")
    T = f"c{tb}_{pt}_{ch}"

    p.add("scalar", lambda s: s.copy(S0Tb[:], S0Tp[pt][:]),
          r=[f"S0T{pt}"], w=["S0Tb"], tag=T + "s0c")
    p.add("vector", lambda v: v.tensor_scalar(Kc[:], KhatF[:, sl], cF[:, c0 + 127:c0 + 128],
                                              None, ALU.mult),
          r=["KhatF", "cF"], w=["Kc"], tag=T + "kc")
    p.add("vector", lambda v: v.tensor_scalar(Bc[:], BhatF[:, sl], cF[:, c0 + 127:c0 + 128],
                                              None, ALU.mult),
          r=["BhatF", "cF"], w=["Bc"], tag=T + "bc")
    p.add("vector", lambda v: v.tensor_scalar(diagp[:, 0:64], ident2[:],
                                              cF[:, c0 + 127:c0 + 128], None, ALU.mult),
          r=["ident2", "cF"], w=["diagp"], tag=T + "dg")
    for nm, srcbuf, srck, dstbuf, dstk, bkx in (
            ("vt", vb, "vb", VTs, "VTs", tb0),
            ("bt", Bc, "Bc", BcT, "BcT", tb1),
            ("kt", Kc, "Kc", KcT, "KcT", tb0)):
        if nm == "vt":
            p.add("tensor", lambda t, bkx=bkx, srcbuf=srcbuf: t.transpose(
                bkx[:, 0:128], srcbuf[:, sl], ident[:]),
                r=[srck, "ident"], w=[bkx.name], tag=T + nm)
        else:
            p.add("tensor", lambda t, bkx=bkx, srcbuf=srcbuf: t.transpose(
                bkx[:, 0:128], srcbuf[:], ident[:]),
                r=[srck, "ident"], w=[bkx.name], tag=T + nm)
        p.add("scalar", lambda s, bkx=bkx, dstbuf=dstbuf: s.copy(dstbuf[:], bkx[:, 0:128]),
              r=[bkx.name], w=[dstk], tag=T + nm + "e")
    for hh in (0, 64):
        hsl = slice(hh, hh + 64)
        prods = (("usb", BhatF, "BhatF", AqF, "AqF", Usb, "Usb", maskSU, "maskSU"),
                 ("usbT", AqF, "AqF", BhatF, "BhatF", UsbT, "UsbT", maskSL, "maskSL"),
                 ("ukb", KhatF, "KhatF", AqF, "AqF", Ukb, "Ukb", maskSU, "maskSU"),
                 ("mbr", BhatF, "BhatF", RqF, "RqF", Mbr, "Mbr", maskIU, "maskIU"),
                 ("mkr", KhatF, "KhatF", RqF, "RqF", Mkr, "Mkr", maskIU, "maskIU"))
        for i, (nm, lb, lk, rb_, rk, ob, ok, mk_, mkk) in enumerate(prods):
            bk = pb[i % 2]
            p.add("tensor", lambda t, lb=lb, rb_=rb_, bk=bk, hsl=hsl:
                  t.matmul(bk[:, 0:128], lb[hsl, sl], rb_[hsl, sl], start=True, stop=True),
                  r=[lk, rk], w=[bk.name], tag=T + nm + str(hh))
            p.add("vector", lambda v, bk=bk, ob=ob, mk_=mk_:
                  v.tensor_mul(ob[:], bk[:, 0:128], mk_[:]),
                  r=[bk.name, mkk], w=[ok], tag=T + nm + "m" + str(hh))
        p.add("vector", lambda v: v.tensor_copy(Ast[:, 0:128], Usb[:]),
              r=["Usb"], w=["Ast"], tag=T + f"a0_{hh}")
        p.add("vector", lambda v: v.tensor_copy(Bst[:, 0:128], UsbT[:]),
              r=["UsbT"], w=["Bst"], tag=T + f"b0_{hh}")
        for k in range(5):
            bA, bB = pb[0], pb[1]
            p.add("tensor", lambda t, k=k, bA=bA: t.matmul(
                bA[:, 0:128], Bst[:, (k % 2) * 128:(k % 2) * 128 + 128],
                Ast[:, k * 128:(k + 1) * 128], start=True, stop=True),
                r=["Ast", "Bst"], w=[bA.name], tag=T + f"sqA{k}_{hh}")
            p.add("tensor", lambda t, k=k, bB=bB: t.matmul(
                bB[:, 0:128], Ast[:, k * 128:(k + 1) * 128],
                Bst[:, (k % 2) * 128:(k % 2) * 128 + 128], start=True, stop=True),
                r=["Ast", "Bst"], w=[bB.name], tag=T + f"sqB{k}_{hh}")
            p.add("scalar", lambda s, k=k, bA=bA: s.copy(
                Ast[:, (k + 1) * 128:(k + 2) * 128], bA[:, 0:128]),
                r=[bA.name], w=["Ast"], tag=T + f"sqAe{k}_{hh}")
            p.add("scalar", lambda s, k=k, bB=bB: s.copy(
                Bst[:, ((k + 1) % 2) * 128:((k + 1) % 2) * 128 + 128], bB[:, 0:128]),
                r=[bB.name], w=["Bst"], tag=T + f"sqBe{k}_{hh}")
        p.add("tensor", lambda t, hsl=hsl: t.matmul(
            wbk[:, 0:64], AqF[hsl, sl], S0Tb[hsl, :], start=True, stop=False),
            r=["AqF", "S0Tb"], w=["wbk"], tag=T + f"w1_{hh}")
        p.add("tensor", lambda t, hh=hh: t.matmul(
            wbk[:, 0:64], Ukb[:], VTs[:, hh:hh + 64], start=False, stop=True),
            r=["Ukb", "VTs"], w=["wbk"], tag=T + f"w2_{hh}")
        p.add("scalar", lambda s: s.copy(Wb[:], wbk[:, 0:64]),
              r=["wbk"], w=["Wb"], tag=T + f"we_{hh}")
        for k in range(6):
            p.add("tensor", lambda t, k=k: t.matmul(
                wbk[:, 0:64], Ast[:, k * 128:(k + 1) * 128], Wb[:],
                start=True, stop=True),
                r=["Ast", "Wb"], w=["wbk"], tag=T + f"wf{k}_{hh}")
            p.add("vector", lambda v: v.tensor_add(Wb[:], Wb[:], wbk[:, 0:64]),
                  r=["wbk", "Wb"], w=["Wb"], tag=T + f"wa{k}_{hh}")
        p.add("vector", lambda v: v.tensor_copy(SAb[:], Wb[:]),
              r=["Wb"], w=["SAb"], tag=T + f"sae_{hh}")
        p.add("tensor", lambda t, hh=hh: t.matmul(
            obk[hh:hh + 64, 0:128], SAb[:], Mbr[:], start=True, stop=False),
            r=["SAb", "Mbr"], w=["obk"], tag=T + f"o1_{hh}")
        p.add("tensor", lambda t, hh=hh: t.matmul(
            obk[hh:hh + 64, 0:128], VTs[:, hh:hh + 64], Mkr[:], start=False, stop=False),
            r=["VTs", "Mkr"], w=["obk"], tag=T + f"o2_{hh}")
        p.add("tensor", lambda t, hh=hh, hsl=hsl: t.matmul(
            obk[hh:hh + 64, 0:128], S0Tb[hsl, :], RqF[hsl, sl], start=False, stop=True),
            r=["S0Tb", "RqF"], w=["obk"], tag=T + f"o3_{hh}")
        p.add("scalar", lambda s, hh=hh: s.copy(OGf[hh:hh + 64, sl], obk[hh:hh + 64, 0:128]),
              r=["obk"], w=["OGf"], tag=T + f"oe_{hh}")
        p.add("tensor", lambda t, hh=hh, hsl=hsl: t.matmul(
            sbk[hh:hh + 64, 0:64], S0Tb[hsl, :], diagp[hsl, 0:64], start=True, stop=False),
            r=["S0Tb", "diagp"], w=["sbk"], tag=T + f"s1_{hh}")
        p.add("tensor", lambda t, hh=hh: t.matmul(
            sbk[hh:hh + 64, 0:64], SAb[:], BcT[:, hh:hh + 64], start=False, stop=False),
            r=["SAb", "BcT"], w=["sbk"], tag=T + f"s2_{hh}")
        p.add("tensor", lambda t, hh=hh: t.matmul(
            sbk[hh:hh + 64, 0:64], VTs[:, hh:hh + 64], KcT[:, hh:hh + 64],
            start=False, stop=True),
            r=["VTs", "KcT"], w=["sbk"], tag=T + f"s3_{hh}")
        p.add("scalar", lambda s, hh=hh: s.copy(SLf[hh:hh + 64, :], sbk[hh:hh + 64, 0:64]),
              r=["sbk"], w=["SLf"], tag=T + f"sle_{hh}")
        p.add("tensor", lambda t, hh=hh: t.matmul(
            xbk[hh:hh + 64, 0:64], SLf[hh:hh + 64, 0:64], identf[hh:hh + 64, hh:hh + 64],
            start=True, stop=True),
            r=["SLf", "identf"], w=["xbk"], tag=T + f"slt_{hh}")
        p.add("scalar", lambda s, hh=hh: s.copy(S0Tp[pt][hh:hh + 64, :], xbk[hh:hh + 64, 0:64]),
              r=["xbk"], w=[f"S0T{pt}"], tag=T + f"sfe_{hh}")


def post_block(p, E, tb):
    g = lambda n: E[n]
    OGf, rb, kpb, vb, ggb = g("OGf"), g("rb"), g("kpb"), g("vb"), g("ggb")
    t1, t2, rowa, rowb, rowc = g("t1"), g("t2"), g("rowa"), g("rowb"), g("rowc")
    onescol, onesrow, CO, xbk, og = g("onescol"), g("onesrow"), g("CO"), g("xbk"), g("og")
    if "_ogo" not in E:
        E["_ogo"] = p.sbuf("ogo", [128, 512], BF16)
    ogo = E["_ogo"]
    T = f"p{tb}_"
    p.add("vector", lambda v: v.tensor_mul(t1[:], OGf[:], OGf[:]),
          r=["OGf"], w=["t1"], tag=T + "sq")
    for f4b in range(4):
        p.add("vector", lambda v, f4b=f4b: v.scalar_tensor_tensor(
            t2[:, f4b * 512:(f4b + 1) * 512], rb[:, f4b * 512:(f4b + 1) * 512],
            CO[:, 24 + f4b:25 + f4b], kpb[:, f4b * 512:(f4b + 1) * 512],
            ALU.mult, ALU.mult),
            r=["rb", "CO", "kpb"], w=["t2"], tag=T + f"bco{f4b}")
    ep = _eps_col(p, 0.00064)
    for f4 in range(4):
        fsl = slice(f4 * 512, (f4 + 1) * 512)
        for hh in (0, 64):
            hsl = slice(hh, hh + 64)
            p.add("tensor", lambda t, f4=f4, hh=hh: t.matmul(
                xbk[0:1, :], onescol[hh:hh + 64, :], OGf[hh:hh + 64, f4 * 512:(f4 + 1) * 512],
                start=True, stop=True),
                r=["OGf", "onescol"], w=["xbk"], tag=T + f"sm{f4}_{hh}")
            p.add("vector", lambda v: v.tensor_copy(rowa[:], xbk[0:1, :]),
                  r=["xbk"], w=["rowa"], tag=T + f"sme{f4}_{hh}")
            p.add("tensor", lambda t, f4=f4, hh=hh: t.matmul(
                xbk[0:1, :], onescol[hh:hh + 64, :], t1[hh:hh + 64, f4 * 512:(f4 + 1) * 512],
                start=True, stop=True),
                r=["t1", "onescol"], w=["xbk"], tag=T + f"sv{f4}_{hh}")
            p.add("vector", lambda v: v.tensor_copy(rowb[:], xbk[0:1, :]),
                  r=["xbk"], w=["rowb"], tag=T + f"sve{f4}_{hh}")
            p.add("vector", lambda v: v.tensor_scalar_mul(rowa[:], rowa[:], 1.0 / 64),
                  r=["rowa"], w=["rowa"], tag=T + f"st1{f4}_{hh}")
            p.add("vector", lambda v: v.tensor_scalar_mul(rowb[:], rowb[:], 1.0 / 64),
                  r=["rowb"], w=["rowb"], tag=T + f"st2{f4}_{hh}")
            p.add("vector", lambda v: v.tensor_mul(rowc[:], rowa[:], rowa[:]),
                  r=["rowa"], w=["rowc"], tag=T + f"st3{f4}_{hh}")
            p.add("vector", lambda v: v.tensor_sub(rowb[:], rowb[:], rowc[:]),
                  r=["rowb", "rowc"], w=["rowb"], tag=T + f"st4{f4}_{hh}")
            p.add("scalar", lambda s: s.activation(rowb[:], rowb[:], AF.Sqrt,
                                                   bias=ep[0:1]),
                  r=["rowb", "epscol0.00064"], w=["rowb"], tag=T + f"st5{f4}_{hh}")
            p.add("vector", lambda v: v.reciprocal(rowb[:], rowb[:]),
                  r=["rowb"], w=["rowb"], tag=T + f"st6{f4}_{hh}")
            p.add("vector", lambda v: v.tensor_mul(rowa[:], rowa[:], rowb[:]),
                  r=["rowa", "rowb"], w=["rowa"], tag=T + f"st7{f4}_{hh}")
            p.add("tensor", lambda t, hh=hh: t.matmul(
                xbk[hh:hh + 64, 0:512], onesrow[:], rowb[:], start=True, stop=True),
                r=["onesrow", "rowb"], w=["xbk"], tag=T + f"br{f4}_{hh}")
            p.add("vector", lambda v, f4=f4, hh=hh: v.tensor_mul(
                OGf[hh:hh + 64, f4 * 512:(f4 + 1) * 512],
                OGf[hh:hh + 64, f4 * 512:(f4 + 1) * 512], xbk[hh:hh + 64, 0:512]),
                r=["xbk", "OGf"], w=["OGf"], tag=T + f"ap1{f4}_{hh}")
            p.add("tensor", lambda t, hh=hh: t.matmul(
                xbk[hh:hh + 64, 0:512], onesrow[:], rowa[:], start=True, stop=True),
                r=["onesrow", "rowa"], w=["xbk"], tag=T + f"bm{f4}_{hh}")
            p.add("vector", lambda v, f4=f4, hh=hh: v.tensor_sub(
                OGf[hh:hh + 64, f4 * 512:(f4 + 1) * 512],
                OGf[hh:hh + 64, f4 * 512:(f4 + 1) * 512], xbk[hh:hh + 64, 0:512]),
                r=["xbk", "OGf"], w=["OGf"], tag=T + f"ap2{f4}_{hh}")
        p.add("vector", lambda v, f4=f4, fsl=fsl: v.tensor_scalar(
            OGf[:, fsl], OGf[:, fsl], CO[:, 16 + f4:17 + f4], CO[:, 20 + f4:21 + f4],
            ALU.mult, ALU.add),
            r=["OGf", "CO"], w=["OGf"], tag=T + f"gb{f4}")
        for hh in (0, 64):
            p.add("tensor", lambda t, f4=f4, hh=hh: t.matmul(
                xbk[0:1, :], onescol[hh:hh + 64, :], t2[hh:hh + 64, f4 * 512:(f4 + 1) * 512],
                start=True, stop=True),
                r=["t2", "onescol"], w=["xbk"], tag=T + f"bc{f4}_{hh}")
            p.add("vector", lambda v: v.tensor_copy(rowc[:], xbk[0:1, :]),
                  r=["xbk"], w=["rowc"], tag=T + f"bce{f4}_{hh}")
            p.add("tensor", lambda t, hh=hh: t.matmul(
                xbk[hh:hh + 64, 0:512], onesrow[:], rowc[:], start=True, stop=True),
                r=["onesrow", "rowc"], w=["xbk"], tag=T + f"bb{f4}_{hh}")
            p.add("vector", lambda v, f4=f4, hh=hh: v.tensor_mul(
                t1[hh:hh + 64, f4 * 512:(f4 + 1) * 512],
                vb[hh:hh + 64, f4 * 512:(f4 + 1) * 512], xbk[hh:hh + 64, 0:512]),
                r=["vb", "xbk"], w=["t1"], tag=T + f"bv{f4}_{hh}")
            p.add("vector", lambda v, f4=f4, hh=hh: v.tensor_add(
                OGf[hh:hh + 64, f4 * 512:(f4 + 1) * 512],
                OGf[hh:hh + 64, f4 * 512:(f4 + 1) * 512],
                t1[hh:hh + 64, f4 * 512:(f4 + 1) * 512]),
                r=["t1", "OGf"], w=["OGf"], tag=T + f"ba{f4}_{hh}")
        p.add("vector", lambda v, fsl=fsl: v.tensor_mul(ogo[:], OGf[:, fsl], ggb[:, fsl]),
              r=["OGf", "ggb"], w=["ogo"], tag=T + f"gt{f4}")
        p.add("sync", lambda s, f4=f4, tb=tb: s.dma_start(
            og[f4 * 128:(f4 + 1) * 128, tb * 512:(tb + 1) * 512], ogo[:]),
            r=["ogo"], w=["ogout"], dma=True, tag=T + f"so{f4}")
from concourse.bass_utils import run_bass_kernel_spmd

MIXIN = ["mr", "mw", "mk", "mv", "ma", "mg"]


def _run(build, in_maps, *args, **kw):
    import gc
    gc.collect()
    nc = bass.Bass("TRN2", target_bir_lowering=False)
    build(nc, *args, **kw)
    return run_bass_kernel_spmd(nc, in_maps, core_ids=list(range(8)))


def kernel(**inputs):
    g = lambda n: np.asarray(inputs[n])
    x0 = np.ascontiguousarray(g("L0_attn_x"), dtype=np.float32)

    w1f, w2f, w3f = g("L0_ffn_w1"), g("L0_ffn_w2"), g("L0_ffn_w3")
    inm_p = [dict(w1s=np.ascontiguousarray(w1f[704 * c:704 * (c + 1)]),
                  w2s=np.ascontiguousarray(w2f[704 * c:704 * (c + 1)]),
                  w3s=np.ascontiguousarray(w3f[:, 704 * c:704 * (c + 1)]))
             for c in range(8)]
    resA = _run(build_prep_a, inm_p)
    parts = np.stack([resA.results[c]["partials"][0] for c in range(8)], axis=1)
    inm_p2 = [dict(inm_p[c], partials=np.ascontiguousarray(parts.reshape(1, 24)))
              for c in range(8)]
    resB = _run(build_prep_b, inm_p2)
    q1t = np.ascontiguousarray(np.concatenate(
        [resB.results[c]["q1t"] for c in range(8)], axis=1))
    q2t = np.ascontiguousarray(np.concatenate(
        [resB.results[c]["q2t"] for c in range(8)], axis=1))
    q3t = np.ascontiguousarray(np.concatenate(
        [resB.results[c]["q3t"] for c in range(8)], axis=0))

    mixc = np.zeros((128, 96), np.float32)
    for j, nme in enumerate(["x_r", "x_w", "x_k", "x_v", "x_a", "x_g"]):
        mixc[:, 16 * j:16 * (j + 1)] = g("L0_attn_" + nme).reshape(16, 128).T
    inm = []
    for c in range(8):
        b, jj = c // 4, c % 4
        xs = np.zeros((2049, 2048), np.float32)
        xs[1:] = x0[b, jj * 2048:(jj + 1) * 2048]
        if jj > 0:
            xs[0] = x0[b, jj * 2048 - 1]
        inm.append(dict(x=xs, mixc=mixc))
    resM = _run(build_l0mix, inm)

    Wr, Wk, Wv = g("L0_attn_W_r"), g("L0_attn_W_k"), g("L0_attn_W_v")
    w1l, w2l = g("L0_attn_w1"), g("L0_attn_w2")
    a1l, a2l = g("L0_attn_a1"), g("L0_attn_a2")
    g1l, g2l = g("L0_attn_g1"), g("L0_attn_g2")
    packs = [g("L0_attn_w0"), g("L0_attn_a0"), g("L0_attn_k_k"), g("L0_attn_k_a"),
             g("L0_attn_ln_gamma"), g("L0_attn_ln_beta"),
             g("L0_attn_r_k").reshape(2048)]
    mixfull = {}
    for b in range(2):
        for mn, on in zip(MIXIN, MIXNAMES):
            mixfull[(b, mn)] = np.ascontiguousarray(np.concatenate(
                [resM.results[4 * b + j][on] for j in range(4)], axis=1))
    inm = []
    for c in range(8):
        b, hq = c // 4, c % 4
        csl = slice(hq * 512, (hq + 1) * 512)
        m = {mn: mixfull[(b, mn)] for mn in MIXIN}
        m.update(wr=np.ascontiguousarray(Wr[:, csl]),
                 wk=np.ascontiguousarray(Wk[:, csl]),
                 wv=np.ascontiguousarray(Wv[:, csl]),
                 w1=w1l, w2=np.ascontiguousarray(w2l[:, csl]),
                 a1=a1l, a2=np.ascontiguousarray(a2l[:, csl]),
                 g1=g1l, g2=np.ascontiguousarray(g2l[:, csl]))
        cols = np.zeros((128, 32), np.float32)
        for i, vec in enumerate(packs):
            cols[:, 4 * i:4 * (i + 1)] = vec[csl].reshape(4, 128).T
        m["cols"] = cols
        inm.append(m)
    resR = _run(build_l0rec, inm)

    OG = [np.concatenate([resR.results[4 * b + hq]["og"] for hq in range(4)], axis=0)
          for b in range(2)]
    inm = [dict(x=np.ascontiguousarray(x0[c // 4, (c % 4) * 2048:(c % 4 + 1) * 2048]),
                at=np.ascontiguousarray(OG[c // 4][:, (c % 4) * 2048:(c % 4 + 1) * 2048]),
                wo=g("L0_attn_W_o"))
           for c in range(8)]
    res5 = _run(build_mixout, inm, "wo", F32)
    x1s = [res5.results[c]["xo"] for c in range(8)]

    inm = [dict(x1=x1s[c], q1t=q1t, q2t=q2t, q3t=q3t) for c in range(8)]
    res6 = _run(build_ffn0, inm)
    x2s = [res6.results[c]["x2"] for c in range(8)]

    Wqa, Wqb = g("L1_attn_Wqa"), g("L1_attn_Wqb")
    Wkva, Wkvb = g("L1_attn_Wkva"), g("L1_attn_Wkvb")
    qperm = []
    for h in range(16):
        base = h * 128
        qperm += [base + 64 + 2 * i for i in range(32)]
        qperm += [base + 64 + 2 * i + 1 for i in range(32)]
        qperm += [base + i for i in range(64)]
    wqbp = np.ascontiguousarray(Wqb[:, qperm])
    kvperm = list(range(512)) + [512 + 2 * i for i in range(32)] + \
             [512 + 2 * i + 1 for i in range(32)]
    wkvap = np.ascontiguousarray(Wkva[:, kvperm])
    kcols, vcols = [], []
    for h in range(16):
        kcols += list(range(h * 192, h * 192 + 64))
        vcols += list(range(h * 192 + 64, h * 192 + 192))
    wkvbk = np.ascontiguousarray(Wkvb[:, kcols])
    wkvbv = np.ascontiguousarray(Wkvb[:, vcols])
    inm = []
    for c in range(8):
        pos = (np.arange(2048, dtype=np.float32) + (c % 4) * 2048).reshape(1, 2048)
        inm.append(dict(x=x2s[c], wqa=Wqa, wqbp=wqbp, wkvap=wkvap,
                        wkvbk=wkvbk, wkvbv=wkvbv, pos=pos))
    res7 = _run(build_l1proj, inm)
    QT = [np.concatenate([res7.results[4 * b + j]["qt"] for j in range(4)], axis=1)
          for b in range(2)]
    KT = [np.concatenate([res7.results[4 * b + j]["kt"] for j in range(4)], axis=1)
          for b in range(2)]
    VV = [np.concatenate([res7.results[4 * b + j]["vv"] for j in range(4)], axis=0)
          for b in range(2)]

    inm, qbss = [], []
    for c in range(8):
        b = c // 4
        m, qbs = attn_host_prep(c, QT[b], KT[b], VV[b])
        inm.append(m)
        qbss.append(qbs)
    res8 = _run(build_attn, inm)
    AO = [np.zeros((2048, 8192), dtype=QT[0].dtype) for _ in range(2)]
    for c in range(8):
        b = c // 4
        ao = res8.results[c]["ao"]
        for s, qb in enumerate(qbss[c]):
            AO[b][:, qb * 512:(qb + 1) * 512] = ao[:, s * 512:(s + 1) * 512]

    inm = [dict(x=x2s[c],
                at=np.ascontiguousarray(AO[c // 4][:, (c % 4) * 2048:(c % 4 + 1) * 2048]),
                wo=g("L1_attn_Wo"))
           for c in range(8)]
    res9 = _run(build_mixout, inm, "wo", F32)
    x3s = [res9.results[c]["xo"] for c in range(8)]

    mut = np.ascontiguousarray(g("L1_ffn_mu").reshape(2048, 2048).T)
    gam = np.ascontiguousarray(g("L1_ffn_gamma").reshape(1, 2048))
    bet = np.ascontiguousarray(g("L1_ffn_beta").reshape(1, 2048))
    scl = np.ascontiguousarray(g("L1_ffn_scale").reshape(1, 1))
    inm = [dict(x3=x3s[c], mut=mut, gamma=gam, beta=bet, scale=scl)
           for c in range(8)]
    res10 = _run(build_l1ffa, inm)

    W1e, B1e = g("L1_ffn_W1"), g("L1_ffn_b1")
    W2e, B2e = g("L1_ffn_W2"), g("L1_ffn_b2")
    inm = []
    for c in range(8):
        b, gq = c // 4, c % 4
        cores_b = [4 * b + i for i in range(4)]
        m = {}
        for i, nme in enumerate(["pa", "pb", "pc", "pd"]):
            m[nme] = np.ascontiguousarray(
                res10.results[cores_b[i]]["sinp"][:, 512 * gq:512 * (gq + 1)])
        den4 = np.stack([res10.results[ci]["denp"][:, 4 * gq:4 * gq + 4]
                         .transpose(1, 0).reshape(512) for ci in cores_b], axis=0)
        m["den4"] = np.ascontiguousarray(den4)
        m["w1e"] = np.ascontiguousarray(np.concatenate(
            [W1e[e] for e in range(4 * gq, 4 * gq + 4)], axis=1))
        m["w2e"] = np.ascontiguousarray(np.concatenate(
            [W2e[e] for e in range(4 * gq, 4 * gq + 4)], axis=0))
        m["b1c"] = np.ascontiguousarray(np.concatenate(
            [B1e[e].reshape(4, 128).T for e in range(4 * gq, 4 * gq + 4)], axis=1))
        m["b2c"] = np.ascontiguousarray(np.concatenate(
            [B2e[e].reshape(16, 128).T for e in range(4 * gq, 4 * gq + 4)], axis=1))
        inm.append(m)
    res11 = _run(build_l1exp, inm)

    inm = []
    for c in range(8):
        b = c // 4
        soT = np.concatenate([res11.results[4 * b + gq]["so"] for gq in range(4)],
                             axis=1)
        inm.append(dict(x=x3s[c],
                        at=np.ascontiguousarray(res10.results[c]["combt"]),
                        som=np.ascontiguousarray(soT.T)))
    res12 = _run(build_mixout, inm, "som", BF16)

    out = np.empty((2, 8192, 2048), dtype=np.float32)
    for c in range(8):
        b, jj = c // 4, c % 4
        out[b, jj * 2048:(jj + 1) * 2048] = res12.results[c]["xo"]
    return out
